# version 17
# baseline (speedup 1.0000x reference)
import contextlib

import numpy as np
import ml_dtypes

import concourse.bass as bass
import concourse.mybir as mybir
from concourse.bass_utils import run_bass_kernel_spmd

F32 = mybir.dt.float32
BF16 = mybir.dt.bfloat16
I32 = mybir.dt.int32
AF = mybir.ActivationFunctionType
ALU = mybir.AluOpType
AX = mybir.AxisListType.X
BFNP = ml_dtypes.bfloat16

NCORES = 8
B_CORE = 4
L = 2048
H = 1024
INNER = 4096
NST = 64
TC = B_CORE * L
MEGA = 1024
NMEGA = TC // MEGA
RPM = MEGA // 128
Q = 128
NCH = L // Q

PI = float(np.pi)
C1 = float(np.float32(6.28125))
C2 = float(np.float32(0.0019302368))
C3 = float(2 * np.pi - np.float64(np.float32(6.28125)) - np.float64(np.float32(0.0019302368)))
INV2PI = 1.0 / (2 * np.pi)
CLAMP = 3.1415925

ENGS = ("sync", "gpsimd", "scalar", "vector", "tensor")


class S:

    NRING = 8

    def __init__(self, nc, ctx):
        self.nc = nc
        self.ev = {k: [] for k in ENGS}
        self.cnt = {}
        self.sems = {}
        for name in ("tv", "vv", "sv", "gv"):
            self.sems[name] = ctx.enter_context(nc.semaphore(name))
            self.cnt[name] = 0
        self.ring_i = {}
        for q in ("qs", "qg", "qa", "qv"):
            self.ring_i[q] = 0
            for i in range(self.NRING):
                n = f"{q}{i}"
                self.sems[n] = ctx.enter_context(nc.semaphore(n))
                self.cnt[n] = 0

    def do(self, eng, fn, waits=(), inc=None, amt=1):
        tok = None
        if inc is not None:
            self.cnt[inc] += amt
            tok = (inc, self.cnt[inc])
        self.ev[eng].append((fn, tuple(w for w in waits if w is not None), inc, amt))
        return tok

    def dma(self, eng, out, in_, waits=(), tr=False):
        q = {"sync": "qs", "gpsimd": "qg", "scalar": "qa", "vector": "qv"}[eng]
        i = self.ring_i[q]
        self.ring_i[q] = (i + 1) % self.NRING
        sem = f"{q}{i}"
        w = list(waits) + ([(sem, self.cnt[sem])] if self.cnt[sem] else [])
        if tr:
            return self.do(eng, (lambda e, o=out, i_=in_: e.dma_start_transpose(o, i_)),
                           w, sem, 16)
        return self.do(eng, (lambda e, o=out, i_=in_: e.dma_start(out=o, in_=i_)),
                       w, sem, 16)

    def wait(self, eng, waits):
        self.do(eng, None, waits)

    def dma_fence(self):
        toks = [(f"{q}{i}", self.cnt[f"{q}{i}"]) for q in ("qs", "qg", "qa", "qv")
                for i in range(self.NRING) if self.cnt[f"{q}{i}"]]
        for eng in ENGS:
            self.wait(eng, toks)

    def flush(self, block):
        sems = self.sems
        for ename in ENGS:
            evs = self.ev[ename]
            if not evs:
                continue

            def body(eng, evs=tuple(evs)):
                lastw = {}
                for fn, waits, inc, amt in evs:
                    for sname, v in waits:
                        if lastw.get(sname, -1) < v:
                            eng.wait_ge(sems[sname], v)
                            lastw[sname] = v
                    if fn is None:
                        continue
                    ins = fn(eng)
                    if inc is not None:
                        ins.then_inc(sems[inc], amt)

            getattr(block, ename)(body)
        self.ev = {k: [] for k in ENGS}


def AP(t, off, dims):
    return bass.AP(t, off, [list(d) for d in dims])


def build_program(upto='all'):
    nc = bass.Bass("TRN2", target_bir_lowering=False)
    dt = nc.dram_tensor

    x_in = dt("x", [TC, H], F32, kind="ExternalInput")
    inp = dt("in_proj", [H, NST], F32, kind="ExternalInput")
    outp = dt("out_proj", [NST, H], F32, kind="ExternalInput")
    freq = dt("frequencies", [1, NST], F32, kind="ExternalInput")
    dec = dt("decays", [1, NST], F32, kind="ExternalInput")
    ssW = dt("ss_W", [H, H], F32, kind="ExternalInput")
    gW = dt("gate_W", [H, H], F32, kind="ExternalInput")
    fW = dt("fin_W", [H, H], F32, kind="ExternalInput")
    ssb = dt("ss_b", [1, H], F32, kind="ExternalInput")
    gb = dt("gate_b", [1, H], F32, kind="ExternalInput")
    fb = dt("fin_b", [1, H], F32, kind="ExternalInput")
    wg = dt("w_gate", [INNER, H], F32, kind="ExternalInput")
    wu = dt("w_up", [INNER, H], F32, kind="ExternalInput")
    wd = dt("w_down", [H, INNER], F32, kind="ExternalInput")
    c_id = dt("c_ident", [128, 128], BF16, kind="ExternalInput")
    c_j = dt("c_jrev", [128, 128], BF16, kind="ExternalInput")
    c_onr = dt("c_ones_r", [1, 128], F32, kind="ExternalInput")
    c_onc = dt("c_ones_c", [128, 1], F32, kind="ExternalInput")
    c_io = dt("c_iotas", [128, 3], F32, kind="ExternalInput")

    y_out = dt("y", [TC, H], F32, kind="ExternalOutput")

    x1b = dt("x1b", [TC, H], F32)
    x2b = dt("x2b", [TC, H], F32)
    ub = dt("ub", [TC, H], BF16)
    cvb = dt("cvb", [TC, H], BF16)
    wgT = dt("wgT", [H, INNER], BF16)
    wuT = dt("wuT", [H, INNER], BF16)
    wdT = dt("wdT", [INNER, H], BF16)
    ssWT = dt("ssWT", [H, H], BF16)
    gWT = dt("gWT", [H, H], BF16)
    fWT = dt("fWT", [H, H], BF16)
    kpad = dt("kpad", [H, 256], BF16)
    scr1 = dt("scr1", [1, 256], F32)
    scr2 = dt("scr2", [64, H], F32)

    ctx = contextlib.ExitStack()
    with ctx:
        sb = lambda name, shape, d=F32: ctx.enter_context(nc.sbuf_tensor(name, shape, d))
        s = S(nc, ctx)

        p_id = sb("p_id", [128, 128], BF16)
        p_j = sb("p_j", [128, 128], BF16)
        p_onr = sb("p_onr", [1, 128])
        p_onc = sb("p_onc", [128, 1])
        p_iota = sb("p_iota", [128, 3])
        p_wsum = sb("p_wsum", [128, 128], BF16)
        p_L = sb("p_L", [128, 128], BF16)
        p_ml = sb("p_ml", [128, 128], BF16)
        p_ct = sb("p_ct", [128, H])
        p_r2 = sb("p_r2", [128, 64])
        p_b2 = sb("p_b2", [128, 64])
        p_mw = sb("p_mw", [128, 4])
        p_eps = sb("p_eps", [128, 1])
        p_mag = sb("p_mag", [128, 2])

        pp = [ctx.enter_context(nc.psum_tensor(f"pp{i}", [128, 1024], F32)) for i in range(4)]
        ps1 = pp[3]

        psum_tok = [None, None, None, None]
        ps1_slots = [None] * 8
        PAIRN = [3]

        def pair(idx):
            i = idx % PAIRN[0]
            return pp[i], i

        def ptok(i):
            return [psum_tok[i]] if psum_tok[i] else []

        def tr_ev(src_ap, dst_ap, waits, dst_waits=(), np_out=128):
            return s.dma("scalar", dst_ap, src_ap, list(waits) + list(dst_waits), tr=True)

        t_id = s.dma("sync", p_id[:, :], c_id.ap())
        t_j = s.dma("sync", p_j[:, :], c_j.ap())
        t_onr = s.dma("sync", p_onr[:, :], c_onr.ap())
        t_onc = s.dma("sync", p_onc[:, :], c_onc.ap())
        t_io = s.dma("sync", p_iota[:, :], c_io.ap())
        t_eps = s.do("gpsimd", lambda e: e.memset(p_eps[:, :], 1e-5), [], "gv")
        MAGIC = float(np.float32(1.5 * 2 ** 23))
        tm1 = s.do("gpsimd", lambda e: e.memset(p_mag[:, 0:1], MAGIC), [], "gv")
        t_mag = s.do("gpsimd", lambda e: e.memset(p_mag[:, 1:2], -MAGIC), [tm1], "gv")

        LVL = {'b0a1': 1, 'b0a2a': 2, 'b0a2b': 3, 'b0a2': 4, 'b0a3a': 5, 'b0a3': 6, 'b0a': 7, 'b0b': 8, 'b0': 9}.get(upto, 99)
        b0 = contextlib.ExitStack()
        with b0:
            sb0 = lambda name, shape, d=F32: b0.enter_context(nc.sbuf_tensor(name, shape, d))
            z_fd = sb0("z_fd", [1, 2 * NST])
            z_fb = sb0("z_fb", [128, 2 * NST])
            z_th = sb0("z_th", [128, NST])
            z_w1 = sb0("z_w1", [128, NST])
            z_w2 = sb0("z_w2", [128, NST])
            z_w3 = sb0("z_w3", [128, NST])
            z_ki = sb0("z_ki", [128, NST], I32)
            z_E = sb0("z_E", [128, NST])
            z_P = sb0("z_P", [128, 6 * NST])
            z_pair = sb0("z_pair", [128, 128], BF16)
            z_ab = sb0("z_ab", [128, 2])
            z_on = sb0("z_on", [128, 128])
            z_d1 = sb0("z_d1", [128, 128])
            z_d2 = sb0("z_d2", [128, 128])
            z_ct64 = sb0("z_ct64", [64, H])
            z_ipT = sb0("z_ipT", [64, H], BF16)
            z_op = sb0("z_op", [64, H])
            z_opb = sb0("z_opb", [64, H], BF16)
            z_ipf = sb0("z_ipf", [128, 8 * 64])
            z_ipb = sb0("z_ipb", [128, 8 * 64], BF16)
            z_p0T = sb0("z_p0T", [64, 128], BF16)
            z_ctb = sb0("z_ctb", [64, H], BF16)
            z_ksl = sb0("z_ksl", [128, 8 * 128], BF16)
            z_zero = sb0("z_zero", [128, 128], BF16)
            z_pad = sb0("z_pad", [128, 128], BF16)
            z_tsc = sb0("z_tsc", [128, 128], BF16)

            tf = s.dma("sync", z_fd[0:1, 0:NST], freq.ap())
            td = s.dma("sync", z_fd[0:1, NST:2 * NST], dec.ap())
            t1 = s.do("vector", lambda e: e.tensor_scalar(out=z_fd[0:1, 0:NST], in0=z_fd[0:1, 0:NST], scalar1=2 * PI, scalar2=None, op0=ALU.mult), [tf], "vv")
            t2 = s.do("scalar", lambda e: e.activation(z_fd[0:1, NST:2 * NST], z_fd[0:1, NST:2 * NST], AF.Exp), [td], "sv")
            t2b = s.do("vector", lambda e: e.tensor_scalar(out=z_fd[0:1, NST:2 * NST], in0=z_fd[0:1, NST:2 * NST], scalar1=-1.0, scalar2=None, op0=ALU.mult), [t2], "vv")
            t3 = s.do("tensor", lambda e: e.matmul(ps1[:, 0:128], p_onr[:, :], z_fd[0:1, 0:128], start=True, stop=True), [t1, t2b, t_onr], "tv")
            t4 = s.do("vector", lambda e: e.tensor_copy(z_fb[:, :], ps1[:, 0:128]), [t3], "vv")

            def reduce_sin(dst, waits):
                a = s.do("vector", lambda e: e.tensor_scalar(out=z_w3[:, :], in0=z_th[:, :], scalar1=INV2PI, scalar2=None, op0=ALU.mult), waits, "vv")
                a = s.do("vector", lambda e: e.tensor_copy(z_ki[:, :], z_w3[:, :]), [a], "vv")
                a = s.do("vector", lambda e: e.tensor_copy(z_w3[:, :], z_ki[:, :]), [a], "vv")
                a = s.do("vector", lambda e: e.scalar_tensor_tensor(out=dst, in0=z_w3[:, :], scalar=-C1, in1=z_th[:, :], op0=ALU.mult, op1=ALU.add), [a], "vv")
                a = s.do("vector", lambda e: e.scalar_tensor_tensor(out=dst, in0=z_w3[:, :], scalar=-C2, in1=dst, op0=ALU.mult, op1=ALU.add), [a], "vv")
                a = s.do("vector", lambda e: e.scalar_tensor_tensor(out=dst, in0=z_w3[:, :], scalar=-C3, in1=dst, op0=ALU.mult, op1=ALU.add), [a], "vv")
                a = s.do("vector", lambda e: e.tensor_scalar(out=z_w3[:, :], in0=dst, scalar1=PI, scalar2=-2 * PI, op0=ALU.is_gt, op1=ALU.mult), [a], "vv")
                a = s.do("vector", lambda e: e.tensor_add(dst, dst, z_w3[:, :]), [a], "vv")
                a = s.do("vector", lambda e: e.tensor_scalar(out=z_w3[:, :], in0=dst, scalar1=-PI, scalar2=2 * PI, op0=ALU.is_lt, op1=ALU.mult), [a], "vv")
                a = s.do("vector", lambda e: e.tensor_add(dst, dst, z_w3[:, :]), [a], "vv")
                a = s.do("vector", lambda e: e.tensor_scalar(out=dst, in0=dst, scalar1=-CLAMP, scalar2=CLAMP, op0=ALU.max, op1=ALU.min), [a], "vv")
                a = s.do("scalar", lambda e: e.activation(dst, dst, AF.Sin), [a], "sv")
                return a

            def build_P(k, ic, waits):
                a0 = s.do("vector", lambda e, ic=ic: e.tensor_scalar(out=z_th[:, :], in0=z_fb[:, 0:NST], scalar1=p_iota[:, ic:ic + 1], scalar2=None, op0=ALU.mult), [t4, t_io] + waits, "vv")
                asin = reduce_sin(z_w1[:, :], [a0])
                b0_ = s.do("vector", lambda e: e.tensor_scalar(out=z_th[:, :], in0=z_th[:, :], scalar1=-1.0, scalar2=PI / 2, op0=ALU.mult, op1=ALU.add), [asin], "vv")
                bcos = reduce_sin(z_w2[:, :], [b0_])
                c0 = s.do("vector", lambda e, ic=ic: e.tensor_scalar(out=z_E[:, :], in0=z_fb[:, NST:2 * NST], scalar1=p_iota[:, ic:ic + 1], scalar2=None, op0=ALU.mult), [t4, t_io] + waits, "vv")
                c1_ = s.do("scalar", lambda e: e.activation(z_E[:, :], z_E[:, :], AF.Exp), [c0], "sv")
                o1 = s.do("vector", lambda e, k=k: e.tensor_mul(z_P[:, 2 * k * NST:(2 * k + 1) * NST], z_E[:, :], z_w2[:, :]), [bcos, c1_], "vv")
                o2 = s.do("vector", lambda e, k=k: e.tensor_mul(z_P[:, (2 * k + 1) * NST:(2 * k + 2) * NST], z_E[:, :], z_w1[:, :]), [asin, c1_, o1], "vv")
                return o2

            tp0 = build_P(0, 0, [])
            tp1 = build_P(1, 1, [tp0])
            tpr = build_P(2, 2, [tp1])

            if LVL >= 2:
              tw1 = s.do("vector", lambda e: e.tensor_copy(p_wsum[:, 0:64], z_P[:, 4 * NST:5 * NST]), [tpr], "vv")
              tw2 = s.do("vector", lambda e: e.tensor_copy(p_wsum[:, 64:128], z_P[:, 5 * NST:6 * NST]), [tpr], "vv")

              tl1 = s.do("vector", lambda e: e.tensor_copy(z_pair[:, 0:64], z_P[:, 2 * NST:3 * NST]), [tp1], "vv")
              tl2 = s.do("vector", lambda e: e.tensor_scalar(out=z_pair[:, 64:128], in0=z_P[:, 3 * NST:4 * NST], scalar1=-1.0, scalar2=None, op0=ALU.mult), [tp1], "vv")
              tl4 = tr_ev(z_pair[:, :], p_L[:, :], [tl1, tl2, t_id])

            if LVL >= 3:
              ta1 = s.dma("gpsimd", scr1.ap()[0:1, 0:64], z_P[127:128, 2 * NST:3 * NST], [tp1])
              ta2 = s.dma("gpsimd", scr1.ap()[0:1, 64:128], z_P[127:128, 3 * NST:4 * NST], [tp1])
              tb1 = s.dma("sync", AP(z_ab, 0, [[2, 64], [1, 1]]), AP(scr1, 0, [[1, 64], [1, 1]]), [ta1])
              tb2 = s.dma("sync", AP(z_ab, 2 * 64, [[2, 64], [1, 1]]), AP(scr1, 0, [[1, 64], [1, 1]]), [ta1])
              tb3 = s.dma("sync", AP(z_ab, 1, [[2, 64], [1, 1]]), AP(scr1, 64, [[1, 64], [1, 1]]), [ta2])
              tb4 = s.dma("sync", AP(z_ab, 2 * 64 + 1, [[2, 64], [1, 1]]), AP(scr1, 64, [[1, 64], [1, 1]]), [ta2])
              tb5 = s.do("vector", lambda e: e.tensor_scalar(out=z_ab[64:128, 1:2], in0=z_ab[64:128, 1:2], scalar1=-1.0, scalar2=None, op0=ALU.mult), [tb4], "vv")
            if LVL >= 4:
              tm0 = s.do("gpsimd", lambda e: e.memset(z_on[:, :], 1.0), [], "gv")
              tm1 = s.do("scalar", lambda e: e.activation(z_d1[:, :], z_on[:, :], AF.Copy, scale=z_ab[:, 0:1]), [tm0, tb1, tb2, tb3], "sv")
              tm2 = s.do("scalar", lambda e: e.activation(z_d2[:, :], z_on[:, :], AF.Copy, scale=z_ab[:, 1:2]), [tm0, tb5, tm1], "sv")
              tm4b = s.do("gpsimd", lambda e: e.affine_select(z_on[:, :], z_d2[:, :], [[1, 128]], ALU.is_equal, 0.0, base=-64, channel_multiplier=-1), [tm2], "gv")
              tm4 = s.do("gpsimd", lambda e: e.affine_select(z_d2[:, :], z_d2[:, :], [[1, 128]], ALU.is_equal, 0.0, base=64, channel_multiplier=-1), [tm4b], "gv")
              tm3 = s.do("gpsimd", lambda e: e.affine_select(z_d1[:, :], z_d1[:, :], [[1, 128]], ALU.is_equal, 0.0, base=0, channel_multiplier=-1), [tm1], "gv")
              tm5 = s.do("vector", lambda e: e.tensor_add(z_d1[:, :], z_d1[:, :], z_d2[:, :]), [tm3, tm4], "vv")
              tm6 = s.do("vector", lambda e: e.tensor_add(p_ml[:, :], z_d1[:, :], z_on[:, :]), [tm5, tm4b], "vv")

            if LVL >= 5:
              tip = s.dma("gpsimd", AP(z_ipb, 0, [[512, 128], [64, 8], [1, 64]]),
                          AP(inp, 0, [[64, 128], [8192, 8], [1, 64]]))
              top = s.dma("sync", z_op[:, :], AP(outp, 0, [[1024, 64], [1, 1024]]))
              topb = s.do("vector", lambda e: e.tensor_copy(z_opb[:, :], z_op[:, :]), [top], "vv")
              tipT = []
              tpz = s.do("gpsimd", lambda e: e.memset(z_pad[:, :], 0.0), [], "gv")
              prev_p = None
              prev_c = None
              for ch in range(8):
                  cp = s.do("vector", lambda e, c=ch: e.tensor_copy(z_pad[:, 0:64], z_ipb[:, c * 64:(c + 1) * 64]),
                            [tip, tpz] + ([prev_t] if ch else []), "vv")
                  prev_t = s.dma("scalar", z_tsc[:, :], z_pad[:, :], [cp] + ([prev_c] if prev_c else []), tr=True)
                  prev_c = s.do("vector", lambda e, c=ch: e.tensor_copy(z_ipT[0:64, c * 128:(c + 1) * 128], z_tsc[0:64, :]), [prev_t], "vv")
                  tipT.append(prev_c)
            if LVL >= 6:
                tct = s.do("vector", lambda e: e.tensor_mul(z_ct64[:, :], z_ipT[0:64, :], z_opb[:, :]), tipT + [topb], "vv")
                tctb = s.do("vector", lambda e: e.tensor_copy(z_ctb[:, :], z_ct64[:, :]), [tct], "vv")
                tc1 = s.do("vector", lambda e: e.tensor_copy(p_ct[0:64, :], z_ct64[:, :]), [tct], "vv")
                tc2a = s.dma("gpsimd", scr2.ap()[0:64, :], z_ct64[:, :], [tct])
                tc2 = s.dma("gpsimd", p_ct[64:128, :], scr2.ap()[0:64, :], [tc2a])

            if LVL >= 7:
              tk0 = s.do("vector", lambda e: e.tensor_copy(z_pad[:, 0:64], z_P[:, 0:NST]),
                         [tpr, tl4, prev_t, prev_c, tpz], "vv")
              tk1 = s.dma("scalar", z_tsc[:, :], z_pad[:, :], [tk0, prev_c], tr=True)
              tk2 = s.do("vector", lambda e: e.tensor_copy(z_p0T[0:64, :], z_tsc[0:64, :]), [tk1], "vv")
              tz = s.do("gpsimd", lambda e: e.memset(z_zero[:, :], 0.0), [], "gv")
              kslab_toks = []
              for ch in range(8):
                  pk, pki = pair(ch)
                  a = s.do("tensor", lambda e, c=ch, pkk=pk: e.matmul(pkk[:, 0:128], z_ctb[:, c * 128:(c + 1) * 128], z_p0T[0:64, :], start=True, stop=True), [tctb, tk2] + ptok(pki), "tv")
                  b = s.do("vector", lambda e, c=ch, pkk=pk: e.tensor_copy(z_ksl[:, c * 128:(c + 1) * 128], pkk[:, 0:128]), [a], "vv")
                  psum_tok[pki] = b
                  kslab_toks.append(b)
              for ch in range(8):
                  s.dma("gpsimd", AP(kpad, ch * 128 * 256, [[256, 128], [1, 128]]), z_zero[:, :], [tz])
                  s.dma("gpsimd", AP(kpad, ch * 128 * 256 + 128, [[256, 128], [1, 128]]),
                        z_ksl[:, ch * 128:(ch + 1) * 128], [kslab_toks[ch]])

            w_full = sb0("w_full", [128, 32 * H])
            w_acc = sb0("w_acc", [128, 40])
            w_sq = sb0("w_sq", [128, H])
            w_qb = sb0("w_qb", [128, H], BF16)
            w_ca = sb0("w_ca", [128, H])
            w_slab = sb0("w_slab", [128, 8192], BF16)
            w_thr = sb0("w_thr", [128, 2])

            def quant_weight(widx, src, dstT, n_tiles, n_cols, carry):
                nseg = n_cols // H
                tl = []
                for i in range(n_tiles):
                    tl.append(s.dma("sync", w_full[:, i * n_cols:(i + 1) * n_cols],
                                    AP(src, i * 128 * n_cols, [[n_cols, 128], [1, n_cols]]), carry))
                rt = []
                for i in range(n_tiles):
                    for seg in range(nseg):
                        cl = i * n_cols + seg * H
                        a = s.do("scalar", lambda e, cl=cl: e.activation(w_sq[:, :], w_full[:, cl:cl + H], AF.Abs), [tl[i]] + ([rt[-1]] if rt else []), "sv")
                        b = s.do("vector", lambda e, k=i * nseg + seg: e.reduce_sum(w_acc[:, k:k + 1], w_sq[:, :], AX), [a], "vv")
                        rt.append(b)
                c = s.do("vector", lambda e, n=n_tiles * nseg: e.reduce_sum(w_acc[:, 36:37], w_acc[:, 0:n], AX), rt, "vv")
                w0 = [ps1_slots[0]] if ps1_slots[0] else []
                d = s.do("tensor", lambda e: e.matmul(ps1[0:1, 0:1], p_onc[:, :], w_acc[:, 36:37], start=True, stop=True), [c, t_onc] + w0, "tv")
                e1 = s.do("vector", lambda e: e.tensor_scalar(out=w_acc[0:1, 37:38], in0=ps1[0:1, 0:1], scalar1=1.0 / (INNER * H), scalar2=1e-5, op0=ALU.mult, op1=ALU.max), [d], "vv")
                f1 = s.do("tensor", lambda e: e.matmul(ps1[:, 4:5], p_onr[:, :], w_acc[0:1, 37:38], start=True, stop=True), [e1, t_onr], "tv")
                f2 = s.do("vector", lambda e, w=widx: e.tensor_copy(p_mw[:, w:w + 1], ps1[:, 4:5]), [f1], "vv")
                ps1_slots[0] = f2
                g1 = s.do("vector", lambda e: e.tensor_scalar(out=w_thr[:, 0:1], in0=p_mw[:, widx:widx + 1], scalar1=0.5, scalar2=None, op0=ALU.mult), [f2], "vv")
                g2 = s.do("vector", lambda e: e.tensor_scalar(out=w_thr[:, 1:2], in0=w_thr[:, 0:1], scalar1=-1.0, scalar2=None, op0=ALU.mult), [g1], "vv")
                prev_slab = None
                last_bq = None
                last_trs = []
                if nseg == 1:
                    gspec = [(g * 128 * 8192, 8192, [(g * 8 + t, 0) for t in range(8)])
                             for g in range(n_tiles // 8)]
                else:
                    gspec = [((seg // 2) * 128 * 16384 + (seg % 2) * 8192, 16384,
                              [(t, seg) for t in range(n_tiles)]) for seg in range(nseg)]
                for goff, pdim, pairs in gspec:
                    trs = []
                    for islot, (i, seg) in enumerate(pairs):
                        cl = i * n_cols + seg * H
                        a = s.do("vector", lambda e, cl=cl: e.tensor_scalar(out=w_ca[:, :], in0=w_full[:, cl:cl + H], scalar1=w_thr[:, 0:1], scalar2=None, op0=ALU.is_ge), [g1] + ([prev_slab] if prev_slab else []) + last_trs, "vv")
                        b = s.do("vector", lambda e, cl=cl: e.scalar_tensor_tensor(out=w_ca[:, :], in0=w_full[:, cl:cl + H], scalar=w_thr[:, 1:2], in1=w_ca[:, :], op0=ALU.is_le, op1=ALU.subtract), [a, g2], "vv")
                        bq = s.do("vector", lambda e: e.tensor_scalar(out=w_qb[:, :], in0=w_ca[:, :], scalar1=-1.0, scalar2=None, op0=ALU.mult), [b], "vv")
                        last_bq = bq
                        trs.append(s.dma("scalar",
                                   AP(w_slab, islot * 128, [[8192, 128], [1024, 8], [1, 128]]),
                                   AP(w_qb, 0, [[1024, 128], [128, 8], [1, 128]]),
                                   [bq] + ([prev_slab] if prev_slab else []) + carry, tr=True))
                        last_trs = trs[-1:]
                    prev_slab = s.dma("gpsimd",
                                      AP(dstT, goff, [[pdim, 128], [1, 8192]]),
                                      AP(w_slab, 0, [[8192, 128], [1, 8192]]),
                                      trs)
                return [prev_slab, last_bq]

            if LVL >= 8:
                cq = quant_weight(0, wg, wgT, 32, H, [])
                cq = quant_weight(1, wu, wuT, 32, H, cq)
                cq = quant_weight(2, wd, wdT, 8, INNER, cq)

            aw_f = sb0("aw_f", [128, H])
            aw_b = sb0("aw_b", [128, H], BF16)

            def attn_wT(src, dstT, carry):
                prev = None
                for i in range(8):
                    a = s.dma("sync", aw_f[:, :], AP(src, i * 128 * H, [[H, 128], [1, H]]), ([prev] if prev else []) + carry)
                    b = s.do("vector", lambda e: e.tensor_copy(aw_b[:, :], aw_f[:, :]), [a] + ([prev] if prev else []) + carry, "vv")
                    trs = [s.dma("scalar",
                           AP(w_slab, 0, [[8192, 128], [128, 8], [1, 128]]),
                           AP(aw_b, 0, [[1024, 128], [128, 8], [1, 128]]),
                           [b] + ([prev] if prev else []) + carry, tr=True)]
                    prev = s.dma("gpsimd",
                                 AP(dstT, i * 128, [[H, 128], [128 * H, 8], [1, 128]]),
                                 AP(w_slab, 0, [[8192, 128], [128, 8], [1, 128]]),
                                 trs)
                return [prev]

            if LVL >= 9:
                cq = attn_wT(ssW, ssWT, cq)
                cq = attn_wT(gW, gWT, cq)
                cq = attn_wT(fW, fWT, cq)

            blk0 = b0.enter_context(nc.Block())
            s.flush(blk0)

        def ffn_phase(src, dst, make_u, tag):
            MEGA2 = 512
            NM2 = TC // MEGA2
            RP2 = MEGA2 // 128
            ph = contextlib.ExitStack()
            with ph:
                s.dma_fence()
                sbp = lambda name, shape, d=F32: ph.enter_context(nc.sbuf_tensor(name + tag, shape, d))
                s_x = sbp("f_x", [128, 2 * RP2 * 1024])
                s_q1T = sbp("f_q1T", [128, 2 * RP2 * 1024], BF16)
                s_pr = sbp("f_pr", [128, RP2 * INNER], BF16)
                s_wb = sbp("f_wb", [128, 4 * 8192], BF16)
                s_sl = sbp("f_sl", [128, 1024], BF16)
                s_u = sbp("f_u", [128, 1024], BF16)
                s_sv = sbp("f_sv", [128, 1024])
                s_sq2 = sbp("f_sq3", [128, 1024])
                s_qTl = [sbp(f"f_qT{i}", [128, 1024], BF16) for i in range(16)]
                st_m1 = sbp("f_m1", [128, 8])
                st_r1 = sbp("f_r1", [128, 8])
                st_A = sbp("f_A", [128, 8])
                st_B = sbp("f_B", [128, 8])
                st_ag = sbp("f_ag", [128, 8])
                st_au = sbp("f_au", [128, 8])
                st_be = sbp("f_be", [128, 8])
                st_s2 = sbp("f_s2", [128, 8])
                st_t1 = sbp("f_t1", [128, 8])
                st_t2 = sbp("f_t2", [128, 8])
                st_sq = sbp("f_sq2", [128, 1])
                st_mx = sbp("f_mx", [128, 32])

                xfree = {0: [], 1: []}
                pending_ln = []
                q1Tfree = {0: None, 1: None}
                prod_guard = {}
                wslot_tok = [None] * 4
                sl_tok = [None]
                qT_tok = [None] * 16
                u_free = None
                sv_free = None
                prev_svr = None
                state = {}

                def bufof(m):
                    return m % 2

                def stc(m, r):
                    return (m % 2) * 4 + r

                def buildA_load(m):
                    bo = bufof(m) * RP2 * 1024
                    tx = s.dma("sync", AP(s_x, bo, [[2 * RP2 * 1024, 128], [1, RP2 * 1024]]),
                               AP(src, m * 128 * RP2 * 1024, [[RP2 * 1024, 128], [1, RP2 * 1024]]),
                               xfree[bufof(m)])
                    xfree[bufof(m)] = []
                    state[m] = {"q1T": {}, "prod": {}, "tx": tx}

                def buildA_rt(m, r):
                    bo = bufof(m) * RP2 * 1024
                    st = state[m]
                    tx = st["tx"]
                    nonlocal sv_free, prev_svr, u_free
                    if True:
                        xs = s_x[:, bo + r * 1024:bo + (r + 1) * 1024]
                        c0 = stc(m, r)
                        a1 = s.do("vector", lambda e, xs=xs, c0=c0: e.reduce_sum(st_m1[:, c0:c0 + 1], xs, AX), [tx], "vv")
                        a2 = s.do("vector", lambda e, xs=xs, c0=c0: e.reduce_max(st_t1[:, c0:c0 + 1], xs, AX), [tx], "vv")
                        a3 = s.do("vector", lambda e, xs=xs, c0=c0: e.tensor_reduce(st_t2[:, c0:c0 + 1], xs, AX, ALU.min), [tx], "vv")
                        sq = s.do("scalar", lambda e, xs=xs: e.activation(s_sq2[:, :], xs, AF.Square), [tx, sv_free], "sv")
                        a4 = s.do("vector", lambda e: e.reduce_sum(st_sq[:, :], s_sq2[:, :], AX), [sq], "vv")
                        sv_free = a4
                        b1 = s.do("vector", lambda e, c0=c0: e.tensor_scalar(out=st_m1[:, c0:c0 + 1], in0=st_m1[:, c0:c0 + 1], scalar1=1.0 / H, scalar2=None, op0=ALU.mult), [a1], "vv")
                        b2 = s.do("vector", lambda e: e.tensor_scalar(out=st_sq[:, :], in0=st_sq[:, :], scalar1=1.0 / H, scalar2=None, op0=ALU.mult), [a4], "vv")
                        b3 = s.do("vector", lambda e, c0=c0: e.scalar_tensor_tensor(out=st_B[:, c0:c0 + 1], in0=st_m1[:, c0:c0 + 1], scalar=st_m1[:, c0:c0 + 1], in1=st_sq[:, :], op0=ALU.mult, op1=ALU.subtract), [b1, b2], "vv")
                        b4 = s.do("scalar", lambda e, c0=c0: e.activation(st_B[:, c0:c0 + 1], st_B[:, c0:c0 + 1], AF.Sqrt, bias=p_eps[:, 0:1], scale=-1.0), [b3, t_eps], "sv")
                        b5 = s.do("vector", lambda e, c0=c0: e.reciprocal(st_r1[:, c0:c0 + 1], st_B[:, c0:c0 + 1]), [b4], "vv")
                        c1_ = s.do("vector", lambda e, c0=c0: e.tensor_sub(st_t1[:, c0:c0 + 1], st_t1[:, c0:c0 + 1], st_m1[:, c0:c0 + 1]), [a2, b1], "vv")
                        c2 = s.do("vector", lambda e, c0=c0: e.tensor_sub(st_t2[:, c0:c0 + 1], st_m1[:, c0:c0 + 1], st_t2[:, c0:c0 + 1]), [a3, b1], "vv")
                        c3 = s.do("vector", lambda e, c0=c0: e.tensor_max(st_t1[:, c0:c0 + 1], st_t1[:, c0:c0 + 1], st_t2[:, c0:c0 + 1]), [c1_, c2], "vv")
                        c4 = s.do("vector", lambda e, c0=c0: e.scalar_tensor_tensor(out=st_t1[:, c0:c0 + 1], in0=st_t1[:, c0:c0 + 1], scalar=1.0, in1=st_r1[:, c0:c0 + 1], op0=ALU.mult, op1=ALU.mult), [c3, b5], "vv")
                        c5 = s.do("vector", lambda e, c0=c0: e.tensor_scalar(out=st_t1[:, c0:c0 + 1], in0=st_t1[:, c0:c0 + 1], scalar1=1e-5, scalar2=None, op0=ALU.max), [c4], "vv")
                        c6 = s.do("vector", lambda e, c0=c0: e.reciprocal(st_t2[:, c0:c0 + 1], st_t1[:, c0:c0 + 1]), [c5], "vv")
                        c7 = s.do("vector", lambda e, c0=c0: e.tensor_scalar(out=st_t2[:, c0:c0 + 1], in0=st_t2[:, c0:c0 + 1], scalar1=127.0, scalar2=None, op0=ALU.mult), [c6], "vv")
                        c8 = s.do("vector", lambda e, c0=c0: e.tensor_mul(st_A[:, c0:c0 + 1], st_r1[:, c0:c0 + 1], st_t2[:, c0:c0 + 1]), [c7, b5], "vv")
                        c9 = s.do("vector", lambda e, c0=c0: e.scalar_tensor_tensor(out=st_B[:, c0:c0 + 1], in0=st_m1[:, c0:c0 + 1], scalar=-1.0, in1=st_A[:, c0:c0 + 1], op0=ALU.mult, op1=ALU.mult), [c8, b1, b4], "vv")
                        d1 = s.do("vector", lambda e, c0=c0: e.tensor_scalar(out=st_t1[:, c0:c0 + 1], in0=st_t1[:, c0:c0 + 1], scalar1=1.0 / 127.0, scalar2=None, op0=ALU.mult), [c5, c8], "vv")
                        d2 = s.do("vector", lambda e, c0=c0: e.tensor_mul(st_ag[:, c0:c0 + 1], st_t1[:, c0:c0 + 1], p_mw[:, 0:1]), [d1], "vv")
                        d3 = s.do("vector", lambda e, c0=c0: e.tensor_mul(st_au[:, c0:c0 + 1], st_t1[:, c0:c0 + 1], p_mw[:, 1:2]), [d1], "vv")
                        e1 = s.do("scalar", lambda e, xs=xs, c0=c0: e.activation(s_sv[:, :], xs, AF.Identity, bias=st_B[:, c0:c0 + 1], scale=st_A[:, c0:c0 + 1]), [c9, prev_svr], "sv")
                        eM = s.do("vector", lambda e: e.tensor_scalar(out=s_sv[:, :], in0=s_sv[:, :], scalar1=MAGIC, scalar2=None, op0=ALU.add), [e1], "vv")
                        e6 = s.do("vector", lambda e: e.tensor_scalar(out=s_u[:, :], in0=s_sv[:, :], scalar1=-MAGIC, scalar2=None, op0=ALU.add), [eM, d3] + ([u_free] if u_free else []), "vv")
                        prev_svr = e6
                        tq = s.dma("scalar",
                                   AP(s_q1T, bo + r * 1024, [[2 * RP2 * 1024, 128], [128, 8], [1, 128]]),
                                   AP(s_u, 0, [[1024, 128], [128, 8], [1, 128]]),
                                   [e6] + ([q1Tfree[bufof(m)]] if q1Tfree[bufof(m)] else []), tr=True)
                        u_free = tq
                        st["q1T"][r] = tq
                        st["prod"][r] = []

                def buildB(m, oqs, rounds_inline=(0, 1, 2, 3)):
                    bo = bufof(m) * RP2 * 1024
                    st = state[m]
                    for oq in oqs:
                        sl_g, sl_u = (0, 1) if oq % 2 == 0 else (2, 3)
                        wg_w = [wslot_tok[sl_g]] if wslot_tok[sl_g] else []
                        wu_w = [wslot_tok[sl_u]] if wslot_tok[sl_u] else []
                        tg = s.dma("sync", AP(s_wb, sl_g * 8192, [[4 * 8192, 128], [1, 8192]]),
                                   AP(wgT, oq * 128 * 8192, [[8192, 128], [1, 8192]]), wg_w)
                        tu = s.dma("sync", AP(s_wb, sl_u * 8192, [[4 * 8192, 128], [1, 8192]]),
                                   AP(wuT, oq * 128 * 8192, [[8192, 128], [1, 8192]]), wu_w)
                        for r in range(RP2):
                            c0 = stc(m, r)
                            pg, pgi = pair(2 * (oq * RP2 + r))
                            pu, pui = pair(2 * (oq * RP2 + r) + 1)
                            mmt = None
                            for j in range(8):
                                lhs = s_q1T[:, bo + r * 1024 + j * 128:bo + r * 1024 + (j + 1) * 128]
                                for half in range(2):
                                    rhs_g = s_wb[:, sl_g * 8192 + j * 1024 + half * 512: sl_g * 8192 + j * 1024 + (half + 1) * 512]
                                    rhs_u = s_wb[:, sl_u * 8192 + j * 1024 + half * 512: sl_u * 8192 + j * 1024 + (half + 1) * 512]
                                    w0 = ([st["q1T"][r], tg, tu] + ptok(pgi) + ptok(pui) + prod_guard.get(m, [])) if (j == 0 and half == 0) else []
                                    if j == 0 and half == 0 and m in prod_guard:
                                        del prod_guard[m]
                                    mm1 = s.do("tensor", lambda e, l=lhs, rh=rhs_g, p=pg, h=half, j=j: e.matmul(p[:, h * 512:(h + 1) * 512], l, rh, start=(j == 0), stop=(j == 7)), w0, "tv")
                                    mmt = s.do("tensor", lambda e, l=lhs, rh=rhs_u, p=pu, h=half, j=j: e.matmul(p[:, h * 512:(h + 1) * 512], l, rh, start=(j == 0), stop=(j == 7)), [], "tv")
                            wslot_tok[sl_g] = mmt
                            wslot_tok[sl_u] = mmt
                            sv1 = s.do("scalar", lambda e, p=pg, c0=c0: e.activation(s_sl[:, :], p[:, :], AF.Silu, scale=st_ag[:, c0:c0 + 1]), [mmt] + ([sl_tok[0]] if sl_tok[0] else []), "sv")
                            pcol = r * INNER + oq * 1024
                            pv1 = s.do("vector", lambda e, p=pu, c0=c0, pc=pcol: e.scalar_tensor_tensor(out=s_pr[:, pc:pc + 1024], in0=p[:, :], scalar=st_au[:, c0:c0 + 1], in1=s_sl[:, :], op0=ALU.mult, op1=ALU.mult), [sv1], "vv")
                            sl_tok[0] = pv1
                            st["prod"][r].append(pv1)
                            psum_tok[pgi] = pv1
                            psum_tok[pui] = pv1
                            if oq == 3 and r in rounds_inline:
                                rounds(m, r)
                            if oq <= 1 and pending_ln and r % 2 == 0:
                                pending_ln.pop(0)()

                def rounds(m, r):
                    nonlocal prev_svr
                    st = state[m]
                    c0 = stc(m, r)
                    f1 = s.do("vector", lambda e, c0=c0, r=r: e.reduce_max(st_t1[:, c0:c0 + 1], s_pr[:, r * INNER:(r + 1) * INNER], AX, apply_absolute_value=True), st["prod"][r], "vv")
                    f2 = s.do("vector", lambda e, c0=c0: e.tensor_scalar(out=st_t1[:, c0:c0 + 1], in0=st_t1[:, c0:c0 + 1], scalar1=1e-5, scalar2=None, op0=ALU.max), [f1], "vv")
                    f3 = s.do("vector", lambda e, c0=c0: e.reciprocal(st_t2[:, c0:c0 + 1], st_t1[:, c0:c0 + 1]), [f2], "vv")
                    f4 = s.do("vector", lambda e, c0=c0: e.tensor_scalar(out=st_s2[:, c0:c0 + 1], in0=st_t2[:, c0:c0 + 1], scalar1=127.0, scalar2=None, op0=ALU.mult), [f3], "vv")
                    f5 = s.do("vector", lambda e, c0=c0: e.tensor_scalar(out=st_t1[:, c0:c0 + 1], in0=st_t1[:, c0:c0 + 1], scalar1=0.5 / 127.0, scalar2=None, op0=ALU.mult), [f4], "vv")
                    f6 = s.do("vector", lambda e, c0=c0: e.tensor_mul(st_be[:, c0:c0 + 1], st_t1[:, c0:c0 + 1], p_mw[:, 2:3]), [f5], "vv")
                    st.setdefault("f6", {})[r] = f6
                    rq = []
                    nonlocal2 = None
                    for hh in range(4):
                        pc = r * INNER + hh * 1024
                        g1 = s.do("scalar", lambda e, pc=pc, c0=c0: e.activation(s_sv[:, :], s_pr[:, pc:pc + 1024], AF.Identity, bias=p_mag[:, 0:1], scale=st_s2[:, c0:c0 + 1]), [f4, prev_svr, t_mag], "sv")
                        g6 = s.do("vector", lambda e, pc=pc: e.tensor_scalar(out=s_pr[:, pc:pc + 1024], in0=s_sv[:, :], scalar1=-MAGIC, scalar2=None, op0=ALU.add), [g1], "vv")
                        prev_svr = g6
                        rq.append(g6)
                    st.setdefault("rq", {})[r] = rq
                    tbs = []
                    for ib in range(4):
                        bi = r * 4 + ib
                        tbs.append(s.dma("scalar",
                                   AP(s_qTl[bi], 0, [[1024, 128], [128, 8], [1, 128]]),
                                   AP(s_pr, r * INNER + ib * 1024, [[RP2 * INNER, 128], [128, 8], [1, 128]]),
                                   [rq[ib]] + ([qT_tok[bi]] if qT_tok[bi] else []), tr=True))
                    st.setdefault("tbs", {})[r] = tbs

                def emit_ln(m, bo, r, c0, tidx, h1):
                    nonlocal u_free, sv_free
                    i1 = s.do("vector", lambda e, bo=bo, r=r, c0=c0: e.reduce_sum(st_t1[:, c0:c0 + 1], s_x[:, bo + r * 1024:bo + (r + 1) * 1024], AX), [h1], "vv")
                    i2a = s.do("scalar", lambda e, bo=bo, r=r: e.activation(s_sq2[:, :], s_x[:, bo + r * 1024:bo + (r + 1) * 1024], AF.Square), [h1, sv_free], "sv")
                    i3 = s.do("vector", lambda e, c0=c0: e.reduce_sum(st_t2[:, c0:c0 + 1], s_sq2[:, :], AX), [i2a], "vv")
                    sv_free = i3
                    i4 = s.do("vector", lambda e, c0=c0: e.tensor_scalar(out=st_t1[:, c0:c0 + 1], in0=st_t1[:, c0:c0 + 1], scalar1=1.0 / H, scalar2=None, op0=ALU.mult), [i1], "vv")
                    i5 = s.do("vector", lambda e, c0=c0: e.tensor_scalar(out=st_t2[:, c0:c0 + 1], in0=st_t2[:, c0:c0 + 1], scalar1=1.0 / H, scalar2=None, op0=ALU.mult), [i3], "vv")
                    i6 = s.do("vector", lambda e, c0=c0: e.scalar_tensor_tensor(out=st_t2[:, c0:c0 + 1], in0=st_t1[:, c0:c0 + 1], scalar=st_t1[:, c0:c0 + 1], in1=st_t2[:, c0:c0 + 1], op0=ALU.mult, op1=ALU.subtract), [i4, i5], "vv")
                    i7 = s.do("scalar", lambda e, c0=c0: e.activation(st_t2[:, c0:c0 + 1], st_t2[:, c0:c0 + 1], AF.Sqrt, bias=p_eps[:, 0:1], scale=-1.0), [i6, t_eps], "sv")
                    i8 = s.do("vector", lambda e, c0=c0, ti=tidx: e.reciprocal(p_r2[:, ti:ti + 1], st_t2[:, c0:c0 + 1]), [i7], "vv")
                    i9 = s.do("vector", lambda e, c0=c0, ti=tidx: e.scalar_tensor_tensor(out=p_b2[:, ti:ti + 1], in0=st_t1[:, c0:c0 + 1], scalar=-1.0, in1=p_r2[:, ti:ti + 1], op0=ALU.mult, op1=ALU.mult), [i8], "vv")
                    j1 = s.do("scalar", lambda e, bo=bo, r=r, ti=tidx: e.activation(s_u[:, :], s_x[:, bo + r * 1024:bo + (r + 1) * 1024], AF.Identity, bias=p_b2[:, ti:ti + 1], scale=p_r2[:, ti:ti + 1]), [i9] + ([u_free] if u_free else []), "sv")
                    u_free = s.dma("gpsimd", AP(ub, (tidx // 32) * 128 * 32768 + (tidx % 32) * 1024, [[32768, 128], [1, 1024]]), s_u[:, :], [j1])
                    xfree[bufof(m)] += [i1, j1]

                def buildC(m, twd1, twd2, post_r=None):
                    nonlocal u_free, sv_free
                    st = state[m]
                    bo = bufof(m) * RP2 * 1024
                    trs_all = []
                    last_mm = None
                    slot_last = [None, None]
                    for r in range(RP2):
                        c0 = stc(m, r)
                        pd, pdi = pair(64 + r)
                        wd_ = ptok(pdi)
                        for ib in range(4):
                            bb = st["tbs"][r][ib]
                            trs_all.append(bb)
                            for k in range(8):
                                i2 = ib * 8 + k
                                for half in range(2):
                                    w0 = ([bb] + wd_ + ([twd1] if i2 == 0 else []) + ([twd2] if i2 == 16 else [])) if (k == 0 and half == 0) or i2 == 16 else []
                                    last_mm = s.do("tensor", lambda e, q=r * 4 + ib, k=k, p=pd, h=half, i2=i2: e.matmul(p[:, h * 512:(h + 1) * 512], s_qTl[q][:, k * 128:(k + 1) * 128], s_wb[:, i2 * 1024 + h * 512:i2 * 1024 + (h + 1) * 512], start=(i2 == 0), stop=(i2 == 31)), w0, "tv")
                                    if r == RP2 - 1 and i2 in (7, 15) and half == 1:
                                        slot_last[i2 // 8] = last_mm
                            qT_tok[r * 4 + ib] = last_mm
                        h1 = s.do("vector", lambda e, p=pd, bo=bo, r=r, c0=c0: e.scalar_tensor_tensor(out=s_x[:, bo + r * 1024:bo + (r + 1) * 1024], in0=p[:, :], scalar=st_be[:, c0:c0 + 1], in1=s_x[:, bo + r * 1024:bo + (r + 1) * 1024], op0=ALU.mult, op1=ALU.add), [last_mm, st["f6"][r]], "vv")
                        psum_tok[pdi] = h1
                        tidx = m * RP2 + r
                        tx1 = s.dma("gpsimd", AP(dst, m * 128 * RP2 * 1024 + r * 1024, [[RP2 * 1024, 128], [1, 1024]]),
                                    s_x[:, bo + r * 1024:bo + (r + 1) * 1024], [h1])
                        xfree[bufof(m)].append(tx1)
                        if make_u:
                            pending_ln.append(lambda bo=bo, r=r, c0=c0, tidx=tidx, h1=h1, m=m: emit_ln(m, bo, r, c0, tidx, h1))
                        if post_r:
                            for fn in post_r.get(r, []):
                                fn()
                    prod_guard[m + 1] = trs_all
                    q1Tfree[bufof(m)] = last_mm
                    wslot_tok[0] = slot_last[0]
                    wslot_tok[1] = slot_last[1]
                    wslot_tok[2] = last_mm
                    wslot_tok[3] = last_mm

                buildA_load(0)
                for r0_ in range(RP2):
                    buildA_rt(0, r0_)
                for m in range(NM2):
                    buildB(m, [0, 1, 2])
                    twd1 = s.dma("sync", AP(s_wb, 0, [[4 * 8192, 128], [1, 16384]]),
                                 AP(wdT, 0, [[16384, 128], [1, 16384]]),
                                 [wslot_tok[0], wslot_tok[1]])
                    buildB(m, [3], rounds_inline=(0, 1))
                    twd2 = s.dma("sync", AP(s_wb, 16 * 1024, [[4 * 8192, 128], [1, 16384]]),
                                 AP(wdT, 128 * 16384, [[16384, 128], [1, 16384]]),
                                 [wslot_tok[2], wslot_tok[3]])
                    if m + 1 < NM2:
                        buildA_load(m + 1)
                    rounds(m, 2)
                    rounds(m, 3)
                    hooks = {}
                    if m + 1 < NM2:
                        hooks = {0: [lambda m=m: buildA_rt(m + 1, 0), lambda m=m: buildA_rt(m + 1, 1)],
                                 1: [lambda m=m: buildA_rt(m + 1, 2), lambda m=m: buildA_rt(m + 1, 3)]}
                    buildC(m, twd1, twd2, post_r=hooks)
                for fn in pending_ln:
                    fn()
                pending_ln.clear()

                blkp = ph.enter_context(nc.Block())
                s.flush(blkp)

        def _fin(touch_y=False):
            fin = contextlib.ExitStack()
            with fin:
                if touch_y:
                    zt = fin.enter_context(nc.sbuf_tensor("fin_z", [128, H], F32))
                    tz0 = s.do("gpsimd", lambda e: e.memset(zt[:, :], 0.0), [], "gv")
                    s.dma("gpsimd", AP(y_out, 0, [[H, 128], [1, H]]), zt[:, :], [tz0])
                s.wait("gpsimd", [(f"qg{i}", s.cnt[f"qg{i}"]) for i in range(S.NRING) if s.cnt[f"qg{i}"]])
                s.wait("sync", [(f"qs{i}", s.cnt[f"qs{i}"]) for i in range(S.NRING) if s.cnt[f"qs{i}"]])
                s.wait("scalar", [(f"qa{i}", s.cnt[f"qa{i}"]) for i in range(S.NRING) if s.cnt[f"qa{i}"]])
                blkf = fin.enter_context(nc.Block())
                s.flush(blkf)

        if upto.startswith('b0'):
            _fin(True)
            return nc

        PAIRN[0] = 4
        ffn_phase(x_in, x1b, True, '1')
        if upto == 'p1':
            _fin(True)
            return nc

        p2 = contextlib.ExitStack()
        with p2:
            s.dma_fence()
            sb2 = lambda name, shape, d=F32: p2.enter_context(nc.sbuf_tensor(name, shape, d))
            s_U = sb2("c_U", [128, 32 * 1024], BF16)
            s_Ur = sb2("c_Ur", [128, 32 * 1024], BF16)
            s_st = sb2("c_st", [128, 2 * 1024], BF16)
            s_ti = sb2("c_ti", [128, 4 * 1024], BF16)
            s_tp = sb2("c_tp", [128, 2 * 1024], BF16)
            s_cv = sb2("c_cv", [128, 2 * 1024], BF16)

            cv_tok = [None, None]
            ti_tok = [None] * 4
            tp_tok = [None, None]
            fence = []

            for g in range(2):
                tU = s.dma("sync", AP(s_U, 0, [[32 * 1024, 128], [1, 32 * 1024]]),
                           AP(ub, g * 128 * 32768, [[32768, 128], [1, 32 * 1024]]),
                           fence)
                jrev = []
                for t in range(32):
                    pj_, pji = pair(t)
                    a = s.do("tensor", lambda e, t=t, p=pj_: e.matmul(p[:, 0:512], p_j[:, :], s_U[:, t * 1024:t * 1024 + 512], start=True, stop=True), [tU, t_j] + ptok(pji) + fence, "tv")
                    a2 = s.do("tensor", lambda e, t=t, p=pj_: e.matmul(p[:, 512:1024], p_j[:, :], s_U[:, t * 1024 + 512:(t + 1) * 1024], start=True, stop=True), [], "tv")
                    b = s.do("vector", lambda e, t=t, p=pj_: e.tensor_copy(s_Ur[:, t * 1024:(t + 1) * 1024], p[:, :]), [a2] + fence, "vv")
                    psum_tok[pji] = b
                    jrev.append(b)
                jdone = jrev[-1]

                ibanks = [(pp[0], 0), (pp[0], 512), (pp[1], 0), (pp[1], 512),
                          (pp[2], 0), (pp[2], 512), (pp[3], 0), (pp[3], 512)]
                intra_done = None
                for hg in range(128):
                    buf = hg % 2
                    tT = s.dma("gpsimd", AP(s_tp, buf * 1024, [[2 * 1024, 128], [128, 8], [1, 128]]),
                               AP(kpad, hg * 8 * 256 + 1, [[1, 128], [256, 8], [1, 128]]),
                               [tp_tok[buf]] if tp_tok[buf] else [])
                    for hi in range(8):
                        h = hg * 8 + hi
                        slot = h % 8
                        pb, off = ibanks[slot]
                        w0 = [tT, jdone] + ([ps1_slots[slot]] if ps1_slots[slot] else [])
                        if hg == 0:
                            w0 += ptok(0) + ptok(1) + ptok(2) + ptok(3)
                        a = s.do("tensor", lambda e, h=h, buf=buf, hi=hi, pb=pb, off=off: e.matmul(
                            pb[:, off:off + 32],
                            s_tp[:, buf * 1024 + hi * 128:buf * 1024 + (hi + 1) * 128],
                            AP(s_Ur, h, [[32 * 1024, 128], [1024, 32]]), start=True, stop=True),
                            w0, "tv")
                        b = s.do("vector", lambda e, h=h, pb=pb, off=off: e.tensor_copy(
                            AP(s_Ur, h, [[32 * 1024, 128], [1024, 32]]), pb[:, off:off + 32]), [a], "vv")
                        ps1_slots[slot] = b
                        intra_done = b
                    tp_tok[buf] = intra_done
                psum_tok[0] = intra_done
                psum_tok[1] = intra_done
                psum_tok[2] = intra_done
                psum_tok[3] = intra_done

                c0_toks = []
                for bl in range(2):
                    tau = bl * NCH
                    gt = (g * 2 + bl) * NCH
                    c0_toks.append(s.dma("gpsimd", AP(cvb, (gt // 8) * 128 * 8192 + (gt % 8) * 1024, [[8192, 128], [1, 1024]]),
                                         s_Ur[:, tau * 1024:(tau + 1) * 1024], [intra_done]))

                sum_last = []
                for bl in range(2):
                    st_tok = None
                    for c in range(NCH - 1):
                        tau = bl * NCH + c
                        pS, pSi = pair(40 + bl)
                        w0 = ptok(pSi)
                        first = (c == 0)
                        if not first:
                            s.do("tensor", lambda e, p=pS, bl=bl: e.matmul(p[:, 0:512], p_ml[:, :], s_st[:, bl * 1024:bl * 1024 + 512], start=True, stop=False), [st_tok] + w0, "tv")
                            s.do("tensor", lambda e, p=pS, bl=bl: e.matmul(p[:, 512:1024], p_ml[:, :], s_st[:, bl * 1024 + 512:(bl + 1) * 1024], start=True, stop=False), [], "tv")
                        s.do("tensor", lambda e, p=pS, tau=tau, first=first: e.matmul(p[:, 0:512], p_wsum[:, :], s_U[:, tau * 1024:tau * 1024 + 512], start=first, stop=True), ([tU] + w0) if first else [tU], "tv")
                        mmt = s.do("tensor", lambda e, p=pS, tau=tau, first=first: e.matmul(p[:, 512:1024], p_wsum[:, :], s_U[:, tau * 1024 + 512:(tau + 1) * 1024], start=first, stop=True), [], "tv")
                        ev1 = s.do("vector", lambda e, p=pS, bl=bl: e.tensor_copy(s_st[:, bl * 1024:(bl + 1) * 1024], p[:, :]), [mmt], "vv")
                        tis = bl * 2 + (c % 2)
                        ev2 = s.do("vector", lambda e, p=pS, t=tis: e.tensor_mul(s_ti[:, t * 1024:(t + 1) * 1024], p_ct[:, :], p[:, :]), [mmt] + ([ti_tok[tis]] if ti_tok[tis] else []), "vv")
                        psum_tok[pSi] = ev2
                        st_tok = ev1
                        cc = c + 1
                        ctau = bl * NCH + cc
                        pC, pCi = pair(42 + (ctau % 2))
                        a = s.do("tensor", lambda e, p=pC, t=tis: e.matmul(p[:, 0:512], p_L[:, :], s_ti[:, t * 1024:t * 1024 + 512], start=True, stop=True), [ev2] + ptok(pCi), "tv")
                        a2 = s.do("tensor", lambda e, p=pC, t=tis: e.matmul(p[:, 512:1024], p_L[:, :], s_ti[:, t * 1024 + 512:(t + 1) * 1024], start=True, stop=True), [], "tv")
                        ti_tok[tis] = a2
                        cs = ctau % 2
                        b = s.do("vector", lambda e, p=pC, ctau=ctau, cs=cs: e.tensor_add(s_cv[:, cs * 1024:(cs + 1) * 1024], p[:, :], s_Ur[:, ctau * 1024:(ctau + 1) * 1024]), [a2, intra_done] + ([cv_tok[cs]] if cv_tok[cs] else []), "vv")
                        psum_tok[pCi] = b
                        gt = (g * 2 + bl) * NCH + cc
                        cv_tok[cs] = s.dma("gpsimd", AP(cvb, (gt // 8) * 128 * 8192 + (gt % 8) * 1024, [[8192, 128], [1, 1024]]),
                                           s_cv[:, cs * 1024:(cs + 1) * 1024], [b])
                    sum_last.append(ev2)

                fence = [cv_tok[0], cv_tok[1]] + c0_toks + sum_last

            blk2 = p2.enter_context(nc.Block())
            s.flush(blk2)

        if upto == 'p2':
            _fin(True)
            return nc

        p3 = contextlib.ExitStack()
        with p3:
            s.dma_fence()
            sb3 = lambda name, shape, d=F32: p3.enter_context(nc.sbuf_tensor(name, shape, d))
            a_x1 = sb3("a_x1", [128, MEGA * 8])
            a_cv = sb3("a_cv", [128, MEGA * 8], BF16)
            a_uT = sb3("a_uT", [128, MEGA * 8], BF16)
            a_cT = sb3("a_cT", [128, MEGA * 8], BF16)
            a_gg = sb3("a_gg", [128, MEGA * 8], BF16)
            a_gs = sb3("a_gs", [128, MEGA * 8], BF16)
            a_w = sb3("a_w", [128, 3 * 8 * 1024], BF16)
            a_bb = sb3("a_bb", [128, 3 * 1024])
            a_u = sb3("a_u", [128, 1024], BF16)
            a_pT = sb3("a_pT", [128, 2 * 1024], BF16)
            a_x2 = sb3("a_x2", [128, 2 * 1024])
            a_t = sb3("a_t", [128, 1024])
            a_br = sb3("a_br", [1, 1024])

            tw_g = s.dma("sync", AP(a_w, 0, [[3 * 8192, 128], [1024, 8], [1, 1024]]),
                         AP(gWT, 0, [[H, 128], [128 * H, 8], [1, H]]))
            tw_s = s.dma("sync", AP(a_w, 8192, [[3 * 8192, 128], [1024, 8], [1, 1024]]),
                         AP(ssWT, 0, [[H, 128], [128 * H, 8], [1, H]]))
            tw_f = s.dma("sync", AP(a_w, 2 * 8192, [[3 * 8192, 128], [1024, 8], [1, 1024]]),
                         AP(fWT, 0, [[H, 128], [128 * H, 8], [1, H]]))
            bias_tok = None
            prev_b = None
            for bi, bsrc in ((0, gb), (1, ssb), (2, fb)):
                tb = s.dma("sync", a_br[0:1, :], bsrc.ap(), [prev_b] if prev_b else [])
                for half in range(2):
                    pB, pBi = pair(bi * 2 + half)
                    a = s.do("tensor", lambda e, p=pB, h=half: e.matmul(p[:, h * 512:(h + 1) * 512], p_onr[:, :], a_br[0:1, h * 512:(h + 1) * 512], start=True, stop=True), [tb, t_onr] + ptok(pBi), "tv")
                    bias_tok = s.do("vector", lambda e, p=pB, bi=bi, h=half: e.tensor_copy(a_bb[:, bi * 1024 + h * 512:bi * 1024 + (h + 1) * 512], p[:, h * 512:(h + 1) * 512]), [a], "vv")
                    psum_tok[pBi] = bias_tok
                prev_b = a

            x1_free = None
            x2_tok = [None, None]
            pT_tok = [None, None]
            prev_uT = []
            prev_r2 = None
            for m in range(NMEGA):
                tx1 = s.dma("sync", AP(a_x1, 0, [[MEGA * 8, 128], [4096, 2], [1, 4096]]),
                            AP(x1b, 2 * m * 128 * 4096, [[4096, 128], [128 * 4096, 2], [1, 4096]]),
                            [x1_free] if x1_free else [])
                tcv = s.dma("sync", AP(a_cv, 0, [[MEGA * 8, 128], [1, 8192]]),
                            AP(cvb, m * 128 * 8192, [[8192, 128], [1, 8192]]),
                            [x1_free] if x1_free else [])
                uT_all = {}
                cT_all = {}
                for r in range(RPM):
                    ti = m * RPM + r
                    j1 = s.do("scalar", lambda e, r=r, ti=ti: e.activation(a_u[:, :], a_x1[:, r * 1024:(r + 1) * 1024], AF.Identity, bias=p_b2[:, ti:ti + 1], scale=p_r2[:, ti:ti + 1]), [tx1] + (uT_all[r - 1] if r else prev_uT), "sv")
                    uT_all[r] = [s.dma("scalar",
                        AP(a_uT, r * 1024, [[MEGA * 8, 128], [128, 8], [1, 128]]),
                        AP(a_u, 0, [[1024, 128], [128, 8], [1, 128]]), [j1], tr=True)]
                    cT_all[r] = [s.dma("scalar",
                        AP(a_cT, r * 1024, [[MEGA * 8, 128], [128, 8], [1, 128]]),
                        AP(a_cv, r * 1024, [[MEGA * 8, 128], [128, 8], [1, 128]]), [tcv], tr=True)]

                for r in range(RPM):
                    pG, pGi = pair(8 + 2 * r)
                    pS2, pSi2 = pair(9 + 2 * r)
                    mmg = mms_ = None
                    for j in range(8):
                        lu = a_uT[:, r * 1024 + j * 128:r * 1024 + (j + 1) * 128]
                        lc = a_cT[:, r * 1024 + j * 128:r * 1024 + (j + 1) * 128]
                        for half in range(2):
                            w0 = (uT_all[r] + cT_all[r] + [tw_g, tw_s] + ptok(pGi) + ptok(pSi2)) if (j == 0 and half == 0) else []
                            mmg = s.do("tensor", lambda e, l=lu, p=pG, j=j, h=half: e.matmul(p[:, h * 512:(h + 1) * 512], l, a_w[:, j * 1024 + h * 512:j * 1024 + (h + 1) * 512], start=(j == 0), stop=(j == 7)), w0, "tv")
                            mms_ = s.do("tensor", lambda e, l=lc, p=pS2, j=j, h=half: e.matmul(p[:, h * 512:(h + 1) * 512], l, a_w[:, 8192 + j * 1024 + h * 512:8192 + j * 1024 + (h + 1) * 512], start=(j == 0), stop=(j == 7)), [], "tv")
                    q1 = s.do("vector", lambda e, p=pG: e.tensor_add(a_t[:, :], p[:, :], a_bb[:, 0:1024]), [mms_, bias_tok, prev_r2], "vv")
                    q2 = s.do("scalar", lambda e, r=r: e.activation(a_gg[:, r * 1024:(r + 1) * 1024], a_t[:, :], AF.Gelu), [q1], "sv")
                    psum_tok[pGi] = q2
                    q3 = s.do("vector", lambda e, p=pS2: e.tensor_add(a_t[:, :], p[:, :], a_bb[:, 1024:2048]), [q2], "vv")
                    q4 = s.do("scalar", lambda e, r=r: e.activation(a_gs[:, r * 1024:(r + 1) * 1024], a_t[:, :], AF.Gelu), [q3], "sv")
                    psum_tok[pSi2] = q4
                    q5 = s.do("vector", lambda e, r=r: e.tensor_mul(a_gg[:, r * 1024:(r + 1) * 1024], a_gg[:, r * 1024:(r + 1) * 1024], a_gs[:, r * 1024:(r + 1) * 1024]), [q4], "vv")
                    pF, pFi = pair(30 + r)
                    mmf = None
                    qs_ = r % 2
                    bb = s.dma("scalar",
                               AP(a_pT, qs_ * 1024, [[2 * 1024, 128], [128, 8], [1, 128]]),
                               AP(a_gg, r * 1024, [[MEGA * 8, 128], [128, 8], [1, 128]]),
                               [q5, tw_f] + ([pT_tok[qs_]] if pT_tok[qs_] else []), tr=True)
                    for j in range(8):
                        for half in range(2):
                            w0 = ([bb] + ptok(pFi)) if (j == 0 and half == 0) else []
                            mmf = s.do("tensor", lambda e, q=qs_, p=pF, h=half, j=j: e.matmul(p[:, h * 512:(h + 1) * 512], a_pT[:, q * 1024 + j * 128:q * 1024 + (j + 1) * 128], a_w[:, 2 * 8192 + j * 1024 + h * 512:2 * 8192 + j * 1024 + (h + 1) * 512], start=(j == 0), stop=(j == 7)), w0, "tv")
                    pT_tok[qs_] = mmf
                    xr = r % 2
                    r1 = s.do("vector", lambda e, p=pF: e.tensor_add(a_t[:, :], p[:, :], a_bb[:, 2048:3072]), [mmf, q4], "vv")
                    r2_ = s.do("vector", lambda e, r=r, xr=xr: e.tensor_add(a_x2[:, xr * 1024:(xr + 1) * 1024], a_t[:, :], a_x1[:, r * 1024:(r + 1) * 1024]), [r1] + ([x2_tok[xr]] if x2_tok[xr] else []), "vv")
                    psum_tok[pFi] = r2_
                    prev_r2 = r2_
                    ti = m * RPM + r
                    x2_tok[xr] = s.dma("gpsimd", AP(x2b, (ti // 4) * 128 * 4096 + (ti % 4) * 1024, [[4096, 128], [1, 1024]]),
                                       a_x2[:, xr * 1024:(xr + 1) * 1024], [r2_])
                x1_free = x2_tok[1] or x2_tok[0]
                prev_uT = uT_all[RPM - 1]

            blk3 = p3.enter_context(nc.Block())
            s.flush(blk3)

        if upto == 'p3a':
            _fin(True)
            return nc

        ffn_phase(x2b, y_out, False, '2')
        _fin()

    return nc


_NC_CACHE = {}


def tile_x(x_core):
    return np.ascontiguousarray(
        x_core.reshape(16, 4, 128, 1024).transpose(0, 2, 1, 3).reshape(TC, H))


def untile_y(y_core):
    return np.ascontiguousarray(
        y_core.reshape(16, 128, 4, 1024).transpose(0, 2, 1, 3).reshape(TC, H))


def kernel(**inputs):
    x = np.ascontiguousarray(np.asarray(inputs["L0_attn_x"], dtype=np.float32))
    assert x.shape == (32, L, H)
    xf = x.reshape(32, L * H)

    if "nc" not in _NC_CACHE:
        _NC_CACHE["nc"] = build_program()
    nc = _NC_CACHE["nc"]

    iotas = np.stack([np.arange(128, dtype=np.float32),
                      np.arange(1, 129, dtype=np.float32),
                      np.arange(127, -1, -1, dtype=np.float32)], axis=1)
    shared = {
        "in_proj": np.ascontiguousarray(inputs["L0_attn_in_proj"], np.float32),
        "out_proj": np.ascontiguousarray(inputs["L0_attn_out_proj"], np.float32),
        "frequencies": np.ascontiguousarray(inputs["L0_attn_frequencies"], np.float32).reshape(1, NST),
        "decays": np.ascontiguousarray(inputs["L0_attn_decays"], np.float32).reshape(1, NST),
        "ss_W": np.ascontiguousarray(inputs["L0_attn_ss_W"], np.float32),
        "gate_W": np.ascontiguousarray(inputs["L0_attn_gate_W"], np.float32),
        "fin_W": np.ascontiguousarray(inputs["L0_attn_fin_W"], np.float32),
        "ss_b": np.ascontiguousarray(inputs["L0_attn_ss_b"], np.float32).reshape(1, H),
        "gate_b": np.ascontiguousarray(inputs["L0_attn_gate_b"], np.float32).reshape(1, H),
        "fin_b": np.ascontiguousarray(inputs["L0_attn_fin_b"], np.float32).reshape(1, H),
        "w_gate": np.ascontiguousarray(inputs["L0_ffn_w_gate"], np.float32),
        "w_up": np.ascontiguousarray(inputs["L0_ffn_w_up"], np.float32),
        "w_down": np.ascontiguousarray(inputs["L0_ffn_w_down"], np.float32),
        "c_ident": np.eye(128, dtype=BFNP),
        "c_jrev": np.eye(128, dtype=BFNP)[::-1].copy(),
        "c_ones_r": np.ones((1, 128), dtype=np.float32),
        "c_ones_c": np.ones((128, 1), dtype=np.float32),
        "c_iotas": np.ascontiguousarray(iotas),
    }
    in_maps = []
    for c in range(NCORES):
        im = dict(shared)
        im["x"] = tile_x(xf[c * B_CORE:(c + 1) * B_CORE].reshape(TC, H))
        in_maps.append(im)

    res = run_bass_kernel_spmd(nc, in_maps, core_ids=list(range(NCORES)))
    out = np.empty((32, L, H), dtype=np.float32)
    for c in range(NCORES):
        out[c * B_CORE:(c + 1) * B_CORE] = untile_y(
            np.asarray(res.results[c]["y"]).reshape(TC, H)).reshape(B_CORE, L, H)
    return out


# version 18
# speedup vs baseline: 1.0744x; 1.0744x over previous
import contextlib

import numpy as np
import ml_dtypes

import concourse.bass as bass
import concourse.mybir as mybir
from concourse.bass_utils import run_bass_kernel_spmd

F32 = mybir.dt.float32
BF16 = mybir.dt.bfloat16
I32 = mybir.dt.int32
AF = mybir.ActivationFunctionType
ALU = mybir.AluOpType
AX = mybir.AxisListType.X
BFNP = ml_dtypes.bfloat16

NCORES = 8
B_CORE = 4
L = 2048
H = 1024
INNER = 4096
NST = 64
TC = B_CORE * L
MEGA = 1024
NMEGA = TC // MEGA
RPM = MEGA // 128
Q = 128
NCH = L // Q

PI = float(np.pi)
C1 = float(np.float32(6.28125))
C2 = float(np.float32(0.0019302368))
C3 = float(2 * np.pi - np.float64(np.float32(6.28125)) - np.float64(np.float32(0.0019302368)))
INV2PI = 1.0 / (2 * np.pi)
CLAMP = 3.1415925

ENGS = ("sync", "gpsimd", "scalar", "vector", "tensor")


class S:

    NRING = 8

    def __init__(self, nc, ctx):
        self.nc = nc
        self.ev = {k: [] for k in ENGS}
        self.cnt = {}
        self.sems = {}
        for name in ("tv", "vv", "sv", "gv"):
            self.sems[name] = ctx.enter_context(nc.semaphore(name))
            self.cnt[name] = 0
        self.ring_i = {}
        for q in ("qs", "qg", "qa", "qv"):
            self.ring_i[q] = 0
            for i in range(self.NRING):
                n = f"{q}{i}"
                self.sems[n] = ctx.enter_context(nc.semaphore(n))
                self.cnt[n] = 0

    def do(self, eng, fn, waits=(), inc=None, amt=1):
        tok = None
        if inc is not None:
            self.cnt[inc] += amt
            tok = (inc, self.cnt[inc])
        self.ev[eng].append((fn, tuple(w for w in waits if w is not None), inc, amt))
        return tok

    def dma(self, eng, out, in_, waits=(), tr=False):
        q = {"sync": "qs", "gpsimd": "qg", "scalar": "qa", "vector": "qv"}[eng]
        i = self.ring_i[q]
        self.ring_i[q] = (i + 1) % self.NRING
        sem = f"{q}{i}"
        w = list(waits) + ([(sem, self.cnt[sem])] if self.cnt[sem] else [])
        if tr:
            return self.do(eng, (lambda e, o=out, i_=in_: e.dma_start_transpose(o, i_)),
                           w, sem, 16)
        return self.do(eng, (lambda e, o=out, i_=in_: e.dma_start(out=o, in_=i_)),
                       w, sem, 16)

    def wait(self, eng, waits):
        self.do(eng, None, waits)

    def dma_fence(self):
        toks = [(f"{q}{i}", self.cnt[f"{q}{i}"]) for q in ("qs", "qg", "qa", "qv")
                for i in range(self.NRING) if self.cnt[f"{q}{i}"]]
        for eng in ENGS:
            self.wait(eng, toks)

    def flush(self, block):
        sems = self.sems
        for ename in ENGS:
            evs = self.ev[ename]
            if not evs:
                continue

            def body(eng, evs=tuple(evs)):
                lastw = {}
                for fn, waits, inc, amt in evs:
                    for sname, v in waits:
                        if lastw.get(sname, -1) < v:
                            eng.wait_ge(sems[sname], v)
                            lastw[sname] = v
                    if fn is None:
                        continue
                    ins = fn(eng)
                    if inc is not None:
                        ins.then_inc(sems[inc], amt)

            getattr(block, ename)(body)
        self.ev = {k: [] for k in ENGS}


def AP(t, off, dims):
    return bass.AP(t, off, [list(d) for d in dims])


def build_program(upto='all'):
    nc = bass.Bass("TRN2", target_bir_lowering=False)
    dt = nc.dram_tensor

    x_in = dt("x", [TC, H], F32, kind="ExternalInput")
    inp = dt("in_proj", [H, NST], F32, kind="ExternalInput")
    outp = dt("out_proj", [NST, H], F32, kind="ExternalInput")
    freq = dt("frequencies", [1, NST], F32, kind="ExternalInput")
    dec = dt("decays", [1, NST], F32, kind="ExternalInput")
    ssW = dt("ss_W", [H, H], F32, kind="ExternalInput")
    gW = dt("gate_W", [H, H], F32, kind="ExternalInput")
    fW = dt("fin_W", [H, H], F32, kind="ExternalInput")
    ssb = dt("ss_b", [1, H], F32, kind="ExternalInput")
    gb = dt("gate_b", [1, H], F32, kind="ExternalInput")
    fb = dt("fin_b", [1, H], F32, kind="ExternalInput")
    wg = dt("w_gate", [INNER, H], F32, kind="ExternalInput")
    wu = dt("w_up", [INNER, H], F32, kind="ExternalInput")
    wd = dt("w_down", [H, INNER], F32, kind="ExternalInput")
    c_id = dt("c_ident", [128, 128], BF16, kind="ExternalInput")
    c_j = dt("c_jrev", [128, 128], BF16, kind="ExternalInput")
    c_onr = dt("c_ones_r", [1, 128], F32, kind="ExternalInput")
    c_onc = dt("c_ones_c", [128, 1], F32, kind="ExternalInput")
    c_io = dt("c_iotas", [128, 3], F32, kind="ExternalInput")

    y_out = dt("y", [TC, H], F32, kind="ExternalOutput")

    x1b = dt("x1b", [TC, H], F32)
    x2b = dt("x2b", [TC, H], F32)
    ub = dt("ub", [TC, H], BF16)
    cvb = dt("cvb", [TC, H], BF16)
    wgT = dt("wgT", [H, INNER], BF16)
    wuT = dt("wuT", [H, INNER], BF16)
    wdT = dt("wdT", [INNER, H], BF16)
    ssWT = dt("ssWT", [H, H], BF16)
    gWT = dt("gWT", [H, H], BF16)
    fWT = dt("fWT", [H, H], BF16)
    kpad = dt("kpad", [H, 256], BF16)
    scr1 = dt("scr1", [1, 256], F32)
    scr2 = dt("scr2", [64, H], F32)

    ctx = contextlib.ExitStack()
    with ctx:
        sb = lambda name, shape, d=F32: ctx.enter_context(nc.sbuf_tensor(name, shape, d))
        s = S(nc, ctx)

        p_id = sb("p_id", [128, 128], BF16)
        p_j = sb("p_j", [128, 128], BF16)
        p_onr = sb("p_onr", [1, 128])
        p_onc = sb("p_onc", [128, 1])
        p_iota = sb("p_iota", [128, 3])
        p_wsum = sb("p_wsum", [128, 128], BF16)
        p_L = sb("p_L", [128, 128], BF16)
        p_ml = sb("p_ml", [128, 128], BF16)
        p_ct = sb("p_ct", [128, H])
        p_r2 = sb("p_r2", [128, 64])
        p_b2 = sb("p_b2", [128, 64])
        p_mw = sb("p_mw", [128, 4])
        p_eps = sb("p_eps", [128, 1])
        p_mag = sb("p_mag", [128, 2])

        pp = [ctx.enter_context(nc.psum_tensor(f"pp{i}", [128, 1024], F32)) for i in range(4)]
        ps1 = pp[3]

        psum_tok = [None, None, None, None]
        ps1_slots = [None] * 8
        PAIRN = [3]

        def pair(idx):
            i = idx % PAIRN[0]
            return pp[i], i

        def ptok(i):
            return [psum_tok[i]] if psum_tok[i] else []

        def tr_ev(src_ap, dst_ap, waits, dst_waits=(), np_out=128):
            return s.dma("scalar", dst_ap, src_ap, list(waits) + list(dst_waits), tr=True)

        t_id = s.dma("sync", p_id[:, :], c_id.ap())
        t_j = s.dma("sync", p_j[:, :], c_j.ap())
        t_onr = s.dma("sync", p_onr[:, :], c_onr.ap())
        t_onc = s.dma("sync", p_onc[:, :], c_onc.ap())
        t_io = s.dma("sync", p_iota[:, :], c_io.ap())
        t_eps = s.do("gpsimd", lambda e: e.memset(p_eps[:, :], 1e-5), [], "gv")
        MAGIC = float(np.float32(1.5 * 2 ** 23))
        tm1 = s.do("gpsimd", lambda e: e.memset(p_mag[:, 0:1], MAGIC), [], "gv")
        t_mag = s.do("gpsimd", lambda e: e.memset(p_mag[:, 1:2], -MAGIC), [tm1], "gv")

        LVL = {'b0a1': 1, 'b0a2a': 2, 'b0a2b': 3, 'b0a2': 4, 'b0a3a': 5, 'b0a3': 6, 'b0a': 7, 'b0b': 8, 'b0': 9}.get(upto, 99)
        b0 = contextlib.ExitStack()
        with b0:
            sb0 = lambda name, shape, d=F32: b0.enter_context(nc.sbuf_tensor(name, shape, d))
            z_fd = sb0("z_fd", [1, 2 * NST])
            z_fb = sb0("z_fb", [128, 2 * NST])
            z_th = sb0("z_th", [128, NST])
            z_w1 = sb0("z_w1", [128, NST])
            z_w2 = sb0("z_w2", [128, NST])
            z_w3 = sb0("z_w3", [128, NST])
            z_ki = sb0("z_ki", [128, NST], I32)
            z_E = sb0("z_E", [128, NST])
            z_P = sb0("z_P", [128, 6 * NST])
            z_pair = sb0("z_pair", [128, 128], BF16)
            z_ab = sb0("z_ab", [128, 2])
            z_on = sb0("z_on", [128, 128])
            z_d1 = sb0("z_d1", [128, 128])
            z_d2 = sb0("z_d2", [128, 128])
            z_ct64 = sb0("z_ct64", [64, H])
            z_ipT = sb0("z_ipT", [64, H], BF16)
            z_op = sb0("z_op", [64, H])
            z_opb = sb0("z_opb", [64, H], BF16)
            z_ipf = sb0("z_ipf", [128, 8 * 64])
            z_ipb = sb0("z_ipb", [128, 8 * 64], BF16)
            z_p0T = sb0("z_p0T", [64, 128], BF16)
            z_ctb = sb0("z_ctb", [64, H], BF16)
            z_ksl = sb0("z_ksl", [128, 8 * 128], BF16)
            z_zero = sb0("z_zero", [128, 128], BF16)
            z_pad = sb0("z_pad", [128, 128], BF16)
            z_tsc = sb0("z_tsc", [128, 128], BF16)

            tf = s.dma("sync", z_fd[0:1, 0:NST], freq.ap())
            td = s.dma("sync", z_fd[0:1, NST:2 * NST], dec.ap())
            t1 = s.do("vector", lambda e: e.tensor_scalar(out=z_fd[0:1, 0:NST], in0=z_fd[0:1, 0:NST], scalar1=2 * PI, scalar2=None, op0=ALU.mult), [tf], "vv")
            t2 = s.do("scalar", lambda e: e.activation(z_fd[0:1, NST:2 * NST], z_fd[0:1, NST:2 * NST], AF.Exp), [td], "sv")
            t2b = s.do("vector", lambda e: e.tensor_scalar(out=z_fd[0:1, NST:2 * NST], in0=z_fd[0:1, NST:2 * NST], scalar1=-1.0, scalar2=None, op0=ALU.mult), [t2], "vv")
            t3 = s.do("tensor", lambda e: e.matmul(ps1[:, 0:128], p_onr[:, :], z_fd[0:1, 0:128], start=True, stop=True), [t1, t2b, t_onr], "tv")
            t4 = s.do("vector", lambda e: e.tensor_copy(z_fb[:, :], ps1[:, 0:128]), [t3], "vv")

            def reduce_sin(dst, waits):
                a = s.do("vector", lambda e: e.tensor_scalar(out=z_w3[:, :], in0=z_th[:, :], scalar1=INV2PI, scalar2=None, op0=ALU.mult), waits, "vv")
                a = s.do("vector", lambda e: e.tensor_copy(z_ki[:, :], z_w3[:, :]), [a], "vv")
                a = s.do("vector", lambda e: e.tensor_copy(z_w3[:, :], z_ki[:, :]), [a], "vv")
                a = s.do("vector", lambda e: e.scalar_tensor_tensor(out=dst, in0=z_w3[:, :], scalar=-C1, in1=z_th[:, :], op0=ALU.mult, op1=ALU.add), [a], "vv")
                a = s.do("vector", lambda e: e.scalar_tensor_tensor(out=dst, in0=z_w3[:, :], scalar=-C2, in1=dst, op0=ALU.mult, op1=ALU.add), [a], "vv")
                a = s.do("vector", lambda e: e.scalar_tensor_tensor(out=dst, in0=z_w3[:, :], scalar=-C3, in1=dst, op0=ALU.mult, op1=ALU.add), [a], "vv")
                a = s.do("vector", lambda e: e.tensor_scalar(out=z_w3[:, :], in0=dst, scalar1=PI, scalar2=-2 * PI, op0=ALU.is_gt, op1=ALU.mult), [a], "vv")
                a = s.do("vector", lambda e: e.tensor_add(dst, dst, z_w3[:, :]), [a], "vv")
                a = s.do("vector", lambda e: e.tensor_scalar(out=z_w3[:, :], in0=dst, scalar1=-PI, scalar2=2 * PI, op0=ALU.is_lt, op1=ALU.mult), [a], "vv")
                a = s.do("vector", lambda e: e.tensor_add(dst, dst, z_w3[:, :]), [a], "vv")
                a = s.do("vector", lambda e: e.tensor_scalar(out=dst, in0=dst, scalar1=-CLAMP, scalar2=CLAMP, op0=ALU.max, op1=ALU.min), [a], "vv")
                a = s.do("scalar", lambda e: e.activation(dst, dst, AF.Sin), [a], "sv")
                return a

            def build_P(k, ic, waits):
                a0 = s.do("vector", lambda e, ic=ic: e.tensor_scalar(out=z_th[:, :], in0=z_fb[:, 0:NST], scalar1=p_iota[:, ic:ic + 1], scalar2=None, op0=ALU.mult), [t4, t_io] + waits, "vv")
                asin = reduce_sin(z_w1[:, :], [a0])
                b0_ = s.do("vector", lambda e: e.tensor_scalar(out=z_th[:, :], in0=z_th[:, :], scalar1=-1.0, scalar2=PI / 2, op0=ALU.mult, op1=ALU.add), [asin], "vv")
                bcos = reduce_sin(z_w2[:, :], [b0_])
                c0 = s.do("vector", lambda e, ic=ic: e.tensor_scalar(out=z_E[:, :], in0=z_fb[:, NST:2 * NST], scalar1=p_iota[:, ic:ic + 1], scalar2=None, op0=ALU.mult), [t4, t_io] + waits, "vv")
                c1_ = s.do("scalar", lambda e: e.activation(z_E[:, :], z_E[:, :], AF.Exp), [c0], "sv")
                o1 = s.do("vector", lambda e, k=k: e.tensor_mul(z_P[:, 2 * k * NST:(2 * k + 1) * NST], z_E[:, :], z_w2[:, :]), [bcos, c1_], "vv")
                o2 = s.do("vector", lambda e, k=k: e.tensor_mul(z_P[:, (2 * k + 1) * NST:(2 * k + 2) * NST], z_E[:, :], z_w1[:, :]), [asin, c1_, o1], "vv")
                return o2

            tp0 = build_P(0, 0, [])
            tp1 = build_P(1, 1, [tp0])
            tpr = build_P(2, 2, [tp1])

            if LVL >= 2:
              tw1 = s.do("vector", lambda e: e.tensor_copy(p_wsum[:, 0:64], z_P[:, 4 * NST:5 * NST]), [tpr], "vv")
              tw2 = s.do("vector", lambda e: e.tensor_copy(p_wsum[:, 64:128], z_P[:, 5 * NST:6 * NST]), [tpr], "vv")

              tl1 = s.do("vector", lambda e: e.tensor_copy(z_pair[:, 0:64], z_P[:, 2 * NST:3 * NST]), [tp1], "vv")
              tl2 = s.do("vector", lambda e: e.tensor_scalar(out=z_pair[:, 64:128], in0=z_P[:, 3 * NST:4 * NST], scalar1=-1.0, scalar2=None, op0=ALU.mult), [tp1], "vv")
              tl4 = tr_ev(z_pair[:, :], p_L[:, :], [tl1, tl2, t_id])

            if LVL >= 3:
              ta1 = s.dma("gpsimd", scr1.ap()[0:1, 0:64], z_P[127:128, 2 * NST:3 * NST], [tp1])
              ta2 = s.dma("gpsimd", scr1.ap()[0:1, 64:128], z_P[127:128, 3 * NST:4 * NST], [tp1])
              tb1 = s.dma("sync", AP(z_ab, 0, [[2, 64], [1, 1]]), AP(scr1, 0, [[1, 64], [1, 1]]), [ta1])
              tb2 = s.dma("sync", AP(z_ab, 2 * 64, [[2, 64], [1, 1]]), AP(scr1, 0, [[1, 64], [1, 1]]), [ta1])
              tb3 = s.dma("sync", AP(z_ab, 1, [[2, 64], [1, 1]]), AP(scr1, 64, [[1, 64], [1, 1]]), [ta2])
              tb4 = s.dma("sync", AP(z_ab, 2 * 64 + 1, [[2, 64], [1, 1]]), AP(scr1, 64, [[1, 64], [1, 1]]), [ta2])
              tb5 = s.do("vector", lambda e: e.tensor_scalar(out=z_ab[64:128, 1:2], in0=z_ab[64:128, 1:2], scalar1=-1.0, scalar2=None, op0=ALU.mult), [tb4], "vv")
            if LVL >= 4:
              tm0 = s.do("gpsimd", lambda e: e.memset(z_on[:, :], 1.0), [], "gv")
              tm1 = s.do("scalar", lambda e: e.activation(z_d1[:, :], z_on[:, :], AF.Copy, scale=z_ab[:, 0:1]), [tm0, tb1, tb2, tb3], "sv")
              tm2 = s.do("scalar", lambda e: e.activation(z_d2[:, :], z_on[:, :], AF.Copy, scale=z_ab[:, 1:2]), [tm0, tb5, tm1], "sv")
              tm4b = s.do("gpsimd", lambda e: e.affine_select(z_on[:, :], z_d2[:, :], [[1, 128]], ALU.is_equal, 0.0, base=-64, channel_multiplier=-1), [tm2], "gv")
              tm4 = s.do("gpsimd", lambda e: e.affine_select(z_d2[:, :], z_d2[:, :], [[1, 128]], ALU.is_equal, 0.0, base=64, channel_multiplier=-1), [tm4b], "gv")
              tm3 = s.do("gpsimd", lambda e: e.affine_select(z_d1[:, :], z_d1[:, :], [[1, 128]], ALU.is_equal, 0.0, base=0, channel_multiplier=-1), [tm1], "gv")
              tm5 = s.do("vector", lambda e: e.tensor_add(z_d1[:, :], z_d1[:, :], z_d2[:, :]), [tm3, tm4], "vv")
              tm6 = s.do("vector", lambda e: e.tensor_add(p_ml[:, :], z_d1[:, :], z_on[:, :]), [tm5, tm4b], "vv")

            if LVL >= 5:
              tip = s.dma("gpsimd", AP(z_ipb, 0, [[512, 128], [64, 8], [1, 64]]),
                          AP(inp, 0, [[64, 128], [8192, 8], [1, 64]]))
              top = s.dma("sync", z_op[:, :], AP(outp, 0, [[1024, 64], [1, 1024]]))
              topb = s.do("vector", lambda e: e.tensor_copy(z_opb[:, :], z_op[:, :]), [top], "vv")
              tipT = []
              tpz = s.do("gpsimd", lambda e: e.memset(z_pad[:, :], 0.0), [], "gv")
              prev_p = None
              prev_c = None
              for ch in range(8):
                  cp = s.do("vector", lambda e, c=ch: e.tensor_copy(z_pad[:, 0:64], z_ipb[:, c * 64:(c + 1) * 64]),
                            [tip, tpz] + ([prev_t] if ch else []), "vv")
                  prev_t = s.dma("scalar", z_tsc[:, :], z_pad[:, :], [cp] + ([prev_c] if prev_c else []), tr=True)
                  prev_c = s.do("vector", lambda e, c=ch: e.tensor_copy(z_ipT[0:64, c * 128:(c + 1) * 128], z_tsc[0:64, :]), [prev_t], "vv")
                  tipT.append(prev_c)
            if LVL >= 6:
                tct = s.do("vector", lambda e: e.tensor_mul(z_ct64[:, :], z_ipT[0:64, :], z_opb[:, :]), tipT + [topb], "vv")
                tctb = s.do("vector", lambda e: e.tensor_copy(z_ctb[:, :], z_ct64[:, :]), [tct], "vv")
                tc1 = s.do("vector", lambda e: e.tensor_copy(p_ct[0:64, :], z_ct64[:, :]), [tct], "vv")
                tc2a = s.dma("gpsimd", scr2.ap()[0:64, :], z_ct64[:, :], [tct])
                tc2 = s.dma("gpsimd", p_ct[64:128, :], scr2.ap()[0:64, :], [tc2a])

            if LVL >= 7:
              tk0 = s.do("vector", lambda e: e.tensor_copy(z_pad[:, 0:64], z_P[:, 0:NST]),
                         [tpr, tl4, prev_t, prev_c, tpz], "vv")
              tk1 = s.dma("scalar", z_tsc[:, :], z_pad[:, :], [tk0, prev_c], tr=True)
              tk2 = s.do("vector", lambda e: e.tensor_copy(z_p0T[0:64, :], z_tsc[0:64, :]), [tk1], "vv")
              tz = s.do("gpsimd", lambda e: e.memset(z_zero[:, :], 0.0), [], "gv")
              kslab_toks = []
              for ch in range(8):
                  pk, pki = pair(ch)
                  a = s.do("tensor", lambda e, c=ch, pkk=pk: e.matmul(pkk[:, 0:128], z_ctb[:, c * 128:(c + 1) * 128], z_p0T[0:64, :], start=True, stop=True), [tctb, tk2] + ptok(pki), "tv")
                  b = s.do("vector", lambda e, c=ch, pkk=pk: e.tensor_copy(z_ksl[:, c * 128:(c + 1) * 128], pkk[:, 0:128]), [a], "vv")
                  psum_tok[pki] = b
                  kslab_toks.append(b)
              for ch in range(8):
                  s.dma("gpsimd", AP(kpad, ch * 128 * 256, [[256, 128], [1, 128]]), z_zero[:, :], [tz])
                  s.dma("gpsimd", AP(kpad, ch * 128 * 256 + 128, [[256, 128], [1, 128]]),
                        z_ksl[:, ch * 128:(ch + 1) * 128], [kslab_toks[ch]])

            w_full = sb0("w_full", [128, 32 * H])
            w_acc = sb0("w_acc", [128, 40])
            w_sq = sb0("w_sq", [128, H])
            w_qb = [sb0("w_qb", [128, H], BF16), sb0("w_qb2", [128, H], BF16)]
            w_ca = sb0("w_ca", [128, H])
            w_slab = sb0("w_slab", [128, 8192], BF16)
            w_thr = sb0("w_thr", [128, 2])

            def quant_weight(widx, src, dstT, n_tiles, n_cols, carry):
                nseg = n_cols // H
                tl = []
                for i in range(n_tiles):
                    tl.append(s.dma("sync", w_full[:, i * n_cols:(i + 1) * n_cols],
                                    AP(src, i * 128 * n_cols, [[n_cols, 128], [1, n_cols]]), carry))
                rt = []
                for i in range(n_tiles):
                    for seg in range(nseg):
                        cl = i * n_cols + seg * H
                        a = s.do("scalar", lambda e, cl=cl, k=i * nseg + seg: e.activation(w_sq[:, :], w_full[:, cl:cl + H], AF.Abs, accum_out=w_acc[:, k:k + 1]), [tl[i]] + ([rt[-1]] if rt else []), "sv")
                        rt.append(a)
                c = s.do("vector", lambda e, n=n_tiles * nseg: e.reduce_sum(w_acc[:, 36:37], w_acc[:, 0:n], AX), rt, "vv")
                w0 = [ps1_slots[0]] if ps1_slots[0] else []
                d = s.do("tensor", lambda e: e.matmul(ps1[0:1, 0:1], p_onc[:, :], w_acc[:, 36:37], start=True, stop=True), [c, t_onc] + w0, "tv")
                e1 = s.do("vector", lambda e: e.tensor_scalar(out=w_acc[0:1, 37:38], in0=ps1[0:1, 0:1], scalar1=1.0 / (INNER * H), scalar2=1e-5, op0=ALU.mult, op1=ALU.max), [d], "vv")
                f1 = s.do("tensor", lambda e: e.matmul(ps1[:, 4:5], p_onr[:, :], w_acc[0:1, 37:38], start=True, stop=True), [e1, t_onr], "tv")
                f2 = s.do("vector", lambda e, w=widx: e.tensor_copy(p_mw[:, w:w + 1], ps1[:, 4:5]), [f1], "vv")
                ps1_slots[0] = f2
                g1 = s.do("vector", lambda e: e.tensor_scalar(out=w_thr[:, 0:1], in0=p_mw[:, widx:widx + 1], scalar1=0.5, scalar2=None, op0=ALU.mult), [f2], "vv")
                g2 = s.do("vector", lambda e: e.tensor_scalar(out=w_thr[:, 1:2], in0=w_thr[:, 0:1], scalar1=-1.0, scalar2=None, op0=ALU.mult), [g1], "vv")
                prev_slab = None
                last_bq = None
                last_bq_l = list(carry)
                qb_trs = {0: list(carry), 1: list(carry)}
                last_trs = []
                if nseg == 1:
                    gspec = [(g * 128 * 8192, 8192, [(g * 8 + t, 0) for t in range(8)])
                             for g in range(n_tiles // 8)]
                else:
                    gspec = [((seg // 2) * 128 * 16384 + (seg % 2) * 8192, 16384,
                              [(t, seg) for t in range(n_tiles)]) for seg in range(nseg)]
                gtile = [0]
                for goff, pdim, pairs in gspec:
                    trs = []
                    for islot, (i, seg) in enumerate(pairs):
                        cl = i * n_cols + seg * H
                        qb = w_qb[gtile[0] % 2]
                        qb_w = qb_trs[gtile[0] % 2]
                        gtile[0] += 1
                        a = s.do("vector", lambda e, cl=cl: e.tensor_scalar(out=w_ca[:, :], in0=w_full[:, cl:cl + H], scalar1=w_thr[:, 1:2], scalar2=None, op0=ALU.is_le), [g2] + last_bq_l, "vv")
                        b = s.do("vector", lambda e, cl=cl, qb=qb: e.scalar_tensor_tensor(out=qb[:, :], in0=w_full[:, cl:cl + H], scalar=w_thr[:, 0:1], in1=w_ca[:, :], op0=ALU.is_ge, op1=ALU.subtract), [a, g1] + (qb_w if qb_w else []) + carry, "vv")
                        last_bq = b
                        last_bq_l = [b]
                        trs.append(s.dma("scalar",
                                   AP(w_slab, islot * 128, [[8192, 128], [1024, 8], [1, 128]]),
                                   AP(qb, 0, [[1024, 128], [128, 8], [1, 128]]),
                                   [b] + ([prev_slab] if prev_slab else []) + carry, tr=True))
                        qb_trs[gtile[0] % 2 ^ 1] = trs[-1:]
                        last_trs = trs[-1:]
                    prev_slab = s.dma("gpsimd",
                                      AP(dstT, goff, [[pdim, 128], [1, 8192]]),
                                      AP(w_slab, 0, [[8192, 128], [1, 8192]]),
                                      trs)
                return [prev_slab, last_bq]

            if LVL >= 8:
                cq = quant_weight(0, wg, wgT, 32, H, [])
                cq = quant_weight(1, wu, wuT, 32, H, cq)
                cq = quant_weight(2, wd, wdT, 8, INNER, cq)

            aw_f = sb0("aw_f", [128, H])
            aw_b = sb0("aw_b", [128, H], BF16)

            def attn_wT(src, dstT, carry):
                prev = None
                for i in range(8):
                    a = s.dma("sync", aw_f[:, :], AP(src, i * 128 * H, [[H, 128], [1, H]]), ([prev] if prev else []) + carry)
                    b = s.do("vector", lambda e: e.tensor_copy(aw_b[:, :], aw_f[:, :]), [a] + ([prev] if prev else []) + carry, "vv")
                    trs = [s.dma("scalar",
                           AP(w_slab, 0, [[8192, 128], [128, 8], [1, 128]]),
                           AP(aw_b, 0, [[1024, 128], [128, 8], [1, 128]]),
                           [b] + ([prev] if prev else []) + carry, tr=True)]
                    prev = s.dma("gpsimd",
                                 AP(dstT, i * 128, [[H, 128], [128 * H, 8], [1, 128]]),
                                 AP(w_slab, 0, [[8192, 128], [128, 8], [1, 128]]),
                                 trs)
                return [prev]

            if LVL >= 9:
                cq = attn_wT(ssW, ssWT, cq)
                cq = attn_wT(gW, gWT, cq)
                cq = attn_wT(fW, fWT, cq)

            blk0 = b0.enter_context(nc.Block())
            s.flush(blk0)

        def ffn_phase(src, dst, make_u, tag):
            MEGA2 = 512
            NM2 = TC // MEGA2
            RP2 = MEGA2 // 128
            ph = contextlib.ExitStack()
            with ph:
                s.dma_fence()
                sbp = lambda name, shape, d=F32: ph.enter_context(nc.sbuf_tensor(name + tag, shape, d))
                s_x = sbp("f_x", [128, 2 * RP2 * 1024])
                s_q1T = sbp("f_q1T", [128, 2 * RP2 * 1024], BF16)
                s_pr = sbp("f_pr", [128, RP2 * INNER], BF16)
                s_wb = sbp("f_wb", [128, 4 * 8192], BF16)
                s_sl = sbp("f_sl", [128, 1024], BF16)
                s_u = sbp("f_u", [128, 1024], BF16)
                s_sv = sbp("f_sv", [128, 1024])
                s_sq2 = sbp("f_sq3", [128, 1024])
                s_qTl = [sbp(f"f_qT{i}", [128, 1024], BF16) for i in range(16)]
                st_m1 = sbp("f_m1", [128, 8])
                st_r1 = sbp("f_r1", [128, 8])
                st_A = sbp("f_A", [128, 8])
                st_B = sbp("f_B", [128, 8])
                st_ag = sbp("f_ag", [128, 8])
                st_au = sbp("f_au", [128, 8])
                st_be = sbp("f_be", [128, 8])
                st_s2 = sbp("f_s2", [128, 8])
                st_t1 = sbp("f_t1", [128, 8])
                st_t2 = sbp("f_t2", [128, 8])
                st_sq = sbp("f_sq2", [128, 1])
                st_mx = sbp("f_mx", [128, 32])

                xfree = {0: [], 1: []}
                pending_ln = []
                q1Tfree = {0: None, 1: None}
                prod_guard = {}
                wslot_tok = [None] * 4
                sl_tok = [None]
                qT_tok = [None] * 16
                u_free = None
                sv_free = None
                prev_svr = None
                state = {}

                def bufof(m):
                    return m % 2

                def stc(m, r):
                    return (m % 2) * 4 + r

                def buildA_load(m):
                    bo = bufof(m) * RP2 * 1024
                    tx = s.dma("sync", AP(s_x, bo, [[2 * RP2 * 1024, 128], [1, RP2 * 1024]]),
                               AP(src, m * 128 * RP2 * 1024, [[RP2 * 1024, 128], [1, RP2 * 1024]]),
                               xfree[bufof(m)])
                    xfree[bufof(m)] = []
                    state[m] = {"q1T": {}, "prod": {}, "tx": tx}

                def buildA_rt(m, r):
                    bo = bufof(m) * RP2 * 1024
                    st = state[m]
                    tx = st["tx"]
                    nonlocal sv_free, prev_svr, u_free
                    if True:
                        xs = s_x[:, bo + r * 1024:bo + (r + 1) * 1024]
                        c0 = stc(m, r)
                        a1 = s.do("vector", lambda e, xs=xs, c0=c0: e.reduce_sum(st_m1[:, c0:c0 + 1], xs, AX), [tx], "vv")
                        a2 = s.do("vector", lambda e, xs=xs, c0=c0: e.reduce_max(st_t1[:, c0:c0 + 1], xs, AX), [tx], "vv")
                        a3 = s.do("vector", lambda e, xs=xs, c0=c0: e.tensor_reduce(st_t2[:, c0:c0 + 1], xs, AX, ALU.min), [tx], "vv")
                        sq = s.do("scalar", lambda e, xs=xs: e.activation(s_sq2[:, :], xs, AF.Square), [tx, sv_free], "sv")
                        a4 = s.do("vector", lambda e: e.reduce_sum(st_sq[:, :], s_sq2[:, :], AX), [sq], "vv")
                        sv_free = a4
                        b1 = s.do("vector", lambda e, c0=c0: e.tensor_scalar(out=st_m1[:, c0:c0 + 1], in0=st_m1[:, c0:c0 + 1], scalar1=1.0 / H, scalar2=None, op0=ALU.mult), [a1], "vv")
                        b2 = s.do("vector", lambda e: e.tensor_scalar(out=st_sq[:, :], in0=st_sq[:, :], scalar1=1.0 / H, scalar2=None, op0=ALU.mult), [a4], "vv")
                        b3 = s.do("vector", lambda e, c0=c0: e.scalar_tensor_tensor(out=st_B[:, c0:c0 + 1], in0=st_m1[:, c0:c0 + 1], scalar=st_m1[:, c0:c0 + 1], in1=st_sq[:, :], op0=ALU.mult, op1=ALU.subtract), [b1, b2], "vv")
                        b4 = s.do("scalar", lambda e, c0=c0: e.activation(st_B[:, c0:c0 + 1], st_B[:, c0:c0 + 1], AF.Sqrt, bias=p_eps[:, 0:1], scale=-1.0), [b3, t_eps], "sv")
                        b5 = s.do("vector", lambda e, c0=c0: e.reciprocal(st_r1[:, c0:c0 + 1], st_B[:, c0:c0 + 1]), [b4], "vv")
                        c1_ = s.do("vector", lambda e, c0=c0: e.tensor_sub(st_t1[:, c0:c0 + 1], st_t1[:, c0:c0 + 1], st_m1[:, c0:c0 + 1]), [a2, b1], "vv")
                        c2 = s.do("vector", lambda e, c0=c0: e.tensor_sub(st_t2[:, c0:c0 + 1], st_m1[:, c0:c0 + 1], st_t2[:, c0:c0 + 1]), [a3, b1], "vv")
                        c3 = s.do("vector", lambda e, c0=c0: e.tensor_max(st_t1[:, c0:c0 + 1], st_t1[:, c0:c0 + 1], st_t2[:, c0:c0 + 1]), [c1_, c2], "vv")
                        c4 = s.do("vector", lambda e, c0=c0: e.scalar_tensor_tensor(out=st_t1[:, c0:c0 + 1], in0=st_t1[:, c0:c0 + 1], scalar=1.0, in1=st_r1[:, c0:c0 + 1], op0=ALU.mult, op1=ALU.mult), [c3, b5], "vv")
                        c5 = s.do("vector", lambda e, c0=c0: e.tensor_scalar(out=st_t1[:, c0:c0 + 1], in0=st_t1[:, c0:c0 + 1], scalar1=1e-5, scalar2=None, op0=ALU.max), [c4], "vv")
                        c6 = s.do("vector", lambda e, c0=c0: e.reciprocal(st_t2[:, c0:c0 + 1], st_t1[:, c0:c0 + 1]), [c5], "vv")
                        c7 = s.do("vector", lambda e, c0=c0: e.tensor_scalar(out=st_t2[:, c0:c0 + 1], in0=st_t2[:, c0:c0 + 1], scalar1=127.0, scalar2=None, op0=ALU.mult), [c6], "vv")
                        c8 = s.do("vector", lambda e, c0=c0: e.tensor_mul(st_A[:, c0:c0 + 1], st_r1[:, c0:c0 + 1], st_t2[:, c0:c0 + 1]), [c7, b5], "vv")
                        c9 = s.do("vector", lambda e, c0=c0: e.scalar_tensor_tensor(out=st_B[:, c0:c0 + 1], in0=st_m1[:, c0:c0 + 1], scalar=-1.0, in1=st_A[:, c0:c0 + 1], op0=ALU.mult, op1=ALU.mult), [c8, b1, b4], "vv")
                        d1 = s.do("vector", lambda e, c0=c0: e.tensor_scalar(out=st_t1[:, c0:c0 + 1], in0=st_t1[:, c0:c0 + 1], scalar1=1.0 / 127.0, scalar2=None, op0=ALU.mult), [c5, c8], "vv")
                        d2 = s.do("vector", lambda e, c0=c0: e.tensor_mul(st_ag[:, c0:c0 + 1], st_t1[:, c0:c0 + 1], p_mw[:, 0:1]), [d1], "vv")
                        d3 = s.do("vector", lambda e, c0=c0: e.tensor_mul(st_au[:, c0:c0 + 1], st_t1[:, c0:c0 + 1], p_mw[:, 1:2]), [d1], "vv")
                        e1 = s.do("scalar", lambda e, xs=xs, c0=c0: e.activation(s_sv[:, :], xs, AF.Identity, bias=st_B[:, c0:c0 + 1], scale=st_A[:, c0:c0 + 1]), [c9, prev_svr], "sv")
                        eM = s.do("vector", lambda e: e.tensor_scalar(out=s_sv[:, :], in0=s_sv[:, :], scalar1=MAGIC, scalar2=None, op0=ALU.add), [e1], "vv")
                        e6 = s.do("vector", lambda e: e.tensor_scalar(out=s_u[:, :], in0=s_sv[:, :], scalar1=-MAGIC, scalar2=None, op0=ALU.add), [eM, d3] + ([u_free] if u_free else []), "vv")
                        prev_svr = e6
                        tq = s.dma("scalar",
                                   AP(s_q1T, bo + r * 1024, [[2 * RP2 * 1024, 128], [128, 8], [1, 128]]),
                                   AP(s_u, 0, [[1024, 128], [128, 8], [1, 128]]),
                                   [e6] + ([q1Tfree[bufof(m)]] if q1Tfree[bufof(m)] else []), tr=True)
                        u_free = tq
                        st["q1T"][r] = tq
                        st["prod"][r] = []

                def buildB(m, oqs, rounds_inline=(0, 1, 2, 3)):
                    bo = bufof(m) * RP2 * 1024
                    st = state[m]
                    for oq in oqs:
                        sl_g, sl_u = (0, 1) if oq % 2 == 0 else (2, 3)
                        wg_w = [wslot_tok[sl_g]] if wslot_tok[sl_g] else []
                        wu_w = [wslot_tok[sl_u]] if wslot_tok[sl_u] else []
                        tg = s.dma("sync", AP(s_wb, sl_g * 8192, [[4 * 8192, 128], [1, 8192]]),
                                   AP(wgT, oq * 128 * 8192, [[8192, 128], [1, 8192]]), wg_w)
                        tu = s.dma("sync", AP(s_wb, sl_u * 8192, [[4 * 8192, 128], [1, 8192]]),
                                   AP(wuT, oq * 128 * 8192, [[8192, 128], [1, 8192]]), wu_w)
                        for r in range(RP2):
                            c0 = stc(m, r)
                            pg, pgi = pair(2 * (oq * RP2 + r))
                            pu, pui = pair(2 * (oq * RP2 + r) + 1)
                            mmt = None
                            for j in range(8):
                                lhs = s_q1T[:, bo + r * 1024 + j * 128:bo + r * 1024 + (j + 1) * 128]
                                for half in range(2):
                                    rhs_g = s_wb[:, sl_g * 8192 + j * 1024 + half * 512: sl_g * 8192 + j * 1024 + (half + 1) * 512]
                                    rhs_u = s_wb[:, sl_u * 8192 + j * 1024 + half * 512: sl_u * 8192 + j * 1024 + (half + 1) * 512]
                                    w0 = ([st["q1T"][r], tg, tu] + ptok(pgi) + ptok(pui) + prod_guard.get(m, [])) if (j == 0 and half == 0) else []
                                    if j == 0 and half == 0 and m in prod_guard:
                                        del prod_guard[m]
                                    mm1 = s.do("tensor", lambda e, l=lhs, rh=rhs_g, p=pg, h=half, j=j: e.matmul(p[:, h * 512:(h + 1) * 512], l, rh, start=(j == 0), stop=(j == 7)), w0, "tv")
                                    mmt = s.do("tensor", lambda e, l=lhs, rh=rhs_u, p=pu, h=half, j=j: e.matmul(p[:, h * 512:(h + 1) * 512], l, rh, start=(j == 0), stop=(j == 7)), [], "tv")
                            wslot_tok[sl_g] = mmt
                            wslot_tok[sl_u] = mmt
                            sv1 = s.do("scalar", lambda e, p=pg, c0=c0: e.activation(s_sl[:, :], p[:, :], AF.Silu, scale=st_ag[:, c0:c0 + 1]), [mmt] + ([sl_tok[0]] if sl_tok[0] else []), "sv")
                            pcol = r * INNER + oq * 1024
                            pv1 = s.do("vector", lambda e, p=pu, c0=c0, pc=pcol: e.scalar_tensor_tensor(out=s_pr[:, pc:pc + 1024], in0=p[:, :], scalar=st_au[:, c0:c0 + 1], in1=s_sl[:, :], op0=ALU.mult, op1=ALU.mult), [sv1], "vv")
                            sl_tok[0] = pv1
                            st["prod"][r].append(pv1)
                            psum_tok[pgi] = pv1
                            psum_tok[pui] = pv1
                            if oq == 3 and r in rounds_inline:
                                rounds(m, r)
                            if oq <= 1 and pending_ln and r % 2 == 1:
                                pending_ln.pop(0)()

                def rounds(m, r):
                    nonlocal prev_svr
                    st = state[m]
                    c0 = stc(m, r)
                    f1 = s.do("vector", lambda e, c0=c0, r=r: e.reduce_max(st_t1[:, c0:c0 + 1], s_pr[:, r * INNER:(r + 1) * INNER], AX, apply_absolute_value=True), st["prod"][r], "vv")
                    f2 = s.do("vector", lambda e, c0=c0: e.tensor_scalar(out=st_t1[:, c0:c0 + 1], in0=st_t1[:, c0:c0 + 1], scalar1=1e-5, scalar2=None, op0=ALU.max), [f1], "vv")
                    f3 = s.do("vector", lambda e, c0=c0: e.reciprocal(st_t2[:, c0:c0 + 1], st_t1[:, c0:c0 + 1]), [f2], "vv")
                    f4 = s.do("vector", lambda e, c0=c0: e.tensor_scalar(out=st_s2[:, c0:c0 + 1], in0=st_t2[:, c0:c0 + 1], scalar1=127.0, scalar2=None, op0=ALU.mult), [f3], "vv")
                    f5 = s.do("vector", lambda e, c0=c0: e.tensor_scalar(out=st_t1[:, c0:c0 + 1], in0=st_t1[:, c0:c0 + 1], scalar1=0.5 / 127.0, scalar2=None, op0=ALU.mult), [f4], "vv")
                    f6 = s.do("vector", lambda e, c0=c0: e.tensor_mul(st_be[:, c0:c0 + 1], st_t1[:, c0:c0 + 1], p_mw[:, 2:3]), [f5], "vv")
                    st.setdefault("f6", {})[r] = f6
                    rq = []
                    nonlocal2 = None
                    for hh in range(4):
                        pc = r * INNER + hh * 1024
                        g1 = s.do("scalar", lambda e, pc=pc, c0=c0: e.activation(s_sv[:, :], s_pr[:, pc:pc + 1024], AF.Identity, bias=p_mag[:, 0:1], scale=st_s2[:, c0:c0 + 1]), [f4, prev_svr, t_mag], "sv")
                        g6 = s.do("vector", lambda e, pc=pc: e.tensor_scalar(out=s_pr[:, pc:pc + 1024], in0=s_sv[:, :], scalar1=-MAGIC, scalar2=None, op0=ALU.add), [g1], "vv")
                        prev_svr = g6
                        rq.append(g6)
                    st.setdefault("rq", {})[r] = rq
                    tbs = []
                    for ib in range(4):
                        bi = r * 4 + ib
                        tbs.append(s.dma("scalar",
                                   AP(s_qTl[bi], 0, [[1024, 128], [128, 8], [1, 128]]),
                                   AP(s_pr, r * INNER + ib * 1024, [[RP2 * INNER, 128], [128, 8], [1, 128]]),
                                   [rq[ib]] + ([qT_tok[bi]] if qT_tok[bi] else []), tr=True))
                    st.setdefault("tbs", {})[r] = tbs

                def emit_ln(m, bo, r, c0, tidx, h1):
                    nonlocal u_free, sv_free
                    i1 = s.do("vector", lambda e, bo=bo, r=r, c0=c0: e.reduce_sum(st_t1[:, c0:c0 + 1], s_x[:, bo + r * 1024:bo + (r + 1) * 1024], AX), [h1], "vv")
                    i2a = s.do("scalar", lambda e, bo=bo, r=r: e.activation(s_sq2[:, :], s_x[:, bo + r * 1024:bo + (r + 1) * 1024], AF.Square), [h1, sv_free], "sv")
                    i3 = s.do("vector", lambda e, c0=c0: e.reduce_sum(st_t2[:, c0:c0 + 1], s_sq2[:, :], AX), [i2a], "vv")
                    sv_free = i3
                    i4 = s.do("vector", lambda e, c0=c0: e.tensor_scalar(out=st_t1[:, c0:c0 + 1], in0=st_t1[:, c0:c0 + 1], scalar1=1.0 / H, scalar2=None, op0=ALU.mult), [i1], "vv")
                    i5 = s.do("vector", lambda e, c0=c0: e.tensor_scalar(out=st_t2[:, c0:c0 + 1], in0=st_t2[:, c0:c0 + 1], scalar1=1.0 / H, scalar2=None, op0=ALU.mult), [i3], "vv")
                    i6 = s.do("vector", lambda e, c0=c0: e.scalar_tensor_tensor(out=st_t2[:, c0:c0 + 1], in0=st_t1[:, c0:c0 + 1], scalar=st_t1[:, c0:c0 + 1], in1=st_t2[:, c0:c0 + 1], op0=ALU.mult, op1=ALU.subtract), [i4, i5], "vv")
                    i7 = s.do("scalar", lambda e, c0=c0: e.activation(st_t2[:, c0:c0 + 1], st_t2[:, c0:c0 + 1], AF.Sqrt, bias=p_eps[:, 0:1], scale=-1.0), [i6, t_eps], "sv")
                    i8 = s.do("vector", lambda e, c0=c0, ti=tidx: e.reciprocal(p_r2[:, ti:ti + 1], st_t2[:, c0:c0 + 1]), [i7], "vv")
                    i9 = s.do("vector", lambda e, c0=c0, ti=tidx: e.scalar_tensor_tensor(out=p_b2[:, ti:ti + 1], in0=st_t1[:, c0:c0 + 1], scalar=-1.0, in1=p_r2[:, ti:ti + 1], op0=ALU.mult, op1=ALU.mult), [i8], "vv")
                    j1 = s.do("scalar", lambda e, bo=bo, r=r, ti=tidx: e.activation(s_u[:, :], s_x[:, bo + r * 1024:bo + (r + 1) * 1024], AF.Identity, bias=p_b2[:, ti:ti + 1], scale=p_r2[:, ti:ti + 1]), [i9] + ([u_free] if u_free else []), "sv")
                    u_free = s.dma("gpsimd", AP(ub, (tidx // 32) * 128 * 32768 + (tidx % 32) * 1024, [[32768, 128], [1, 1024]]), s_u[:, :], [j1])
                    xfree[bufof(m)] += [i1, j1]

                def buildC(m, twd1, twd2, post_r=None):
                    nonlocal u_free, sv_free
                    st = state[m]
                    bo = bufof(m) * RP2 * 1024
                    trs_all = []
                    last_mm = None
                    slot_last = [None, None]
                    for r in range(RP2):
                        c0 = stc(m, r)
                        pd, pdi = pair(64 + r)
                        wd_ = ptok(pdi)
                        for ib in range(4):
                            bb = st["tbs"][r][ib]
                            trs_all.append(bb)
                            for k in range(8):
                                i2 = ib * 8 + k
                                for half in range(2):
                                    w0 = ([bb] + wd_ + ([twd1] if i2 == 0 else []) + ([twd2] if i2 == 16 else [])) if (k == 0 and half == 0) or i2 == 16 else []
                                    last_mm = s.do("tensor", lambda e, q=r * 4 + ib, k=k, p=pd, h=half, i2=i2: e.matmul(p[:, h * 512:(h + 1) * 512], s_qTl[q][:, k * 128:(k + 1) * 128], s_wb[:, i2 * 1024 + h * 512:i2 * 1024 + (h + 1) * 512], start=(i2 == 0), stop=(i2 == 31)), w0, "tv")
                                    if r == RP2 - 1 and i2 in (7, 15) and half == 1:
                                        slot_last[i2 // 8] = last_mm
                            qT_tok[r * 4 + ib] = last_mm
                        h1 = s.do("vector", lambda e, p=pd, bo=bo, r=r, c0=c0: e.scalar_tensor_tensor(out=s_x[:, bo + r * 1024:bo + (r + 1) * 1024], in0=p[:, :], scalar=st_be[:, c0:c0 + 1], in1=s_x[:, bo + r * 1024:bo + (r + 1) * 1024], op0=ALU.mult, op1=ALU.add), [last_mm, st["f6"][r]], "vv")
                        psum_tok[pdi] = h1
                        tidx = m * RP2 + r
                        tx1 = s.dma("gpsimd", AP(dst, m * 128 * RP2 * 1024 + r * 1024, [[RP2 * 1024, 128], [1, 1024]]),
                                    s_x[:, bo + r * 1024:bo + (r + 1) * 1024], [h1])
                        xfree[bufof(m)].append(tx1)
                        if make_u:
                            pending_ln.append(lambda bo=bo, r=r, c0=c0, tidx=tidx, h1=h1, m=m: emit_ln(m, bo, r, c0, tidx, h1))
                        if post_r:
                            for fn in post_r.get(r, []):
                                fn()
                    prod_guard[m + 1] = trs_all
                    q1Tfree[bufof(m)] = last_mm
                    wslot_tok[0] = slot_last[0]
                    wslot_tok[1] = slot_last[1]
                    wslot_tok[2] = last_mm
                    wslot_tok[3] = last_mm

                buildA_load(0)
                for r0_ in range(RP2):
                    buildA_rt(0, r0_)
                for m in range(NM2):
                    buildB(m, [0, 1, 2])
                    if m + 1 < NM2:
                        buildA_load(m + 1)
                    twd1 = s.dma("sync", AP(s_wb, 0, [[4 * 8192, 128], [1, 16384]]),
                                 AP(wdT, 0, [[16384, 128], [1, 16384]]),
                                 [wslot_tok[0], wslot_tok[1]])
                    buildB(m, [3], rounds_inline=(0, 1))
                    twd2 = s.dma("sync", AP(s_wb, 16 * 1024, [[4 * 8192, 128], [1, 16384]]),
                                 AP(wdT, 128 * 16384, [[16384, 128], [1, 16384]]),
                                 [wslot_tok[2], wslot_tok[3]])
                    rounds(m, 2)
                    rounds(m, 3)
                    hooks = {}
                    if m + 1 < NM2:
                        hooks = {0: [lambda m=m: buildA_rt(m + 1, 0), lambda m=m: buildA_rt(m + 1, 1)],
                                 1: [lambda m=m: buildA_rt(m + 1, 2), lambda m=m: buildA_rt(m + 1, 3)]}
                    buildC(m, twd1, twd2, post_r=hooks)
                for fn in pending_ln:
                    fn()
                pending_ln.clear()

                blkp = ph.enter_context(nc.Block())
                s.flush(blkp)

        def _fin(touch_y=False):
            fin = contextlib.ExitStack()
            with fin:
                if touch_y:
                    zt = fin.enter_context(nc.sbuf_tensor("fin_z", [128, H], F32))
                    tz0 = s.do("gpsimd", lambda e: e.memset(zt[:, :], 0.0), [], "gv")
                    s.dma("gpsimd", AP(y_out, 0, [[H, 128], [1, H]]), zt[:, :], [tz0])
                s.wait("gpsimd", [(f"qg{i}", s.cnt[f"qg{i}"]) for i in range(S.NRING) if s.cnt[f"qg{i}"]])
                s.wait("sync", [(f"qs{i}", s.cnt[f"qs{i}"]) for i in range(S.NRING) if s.cnt[f"qs{i}"]])
                s.wait("scalar", [(f"qa{i}", s.cnt[f"qa{i}"]) for i in range(S.NRING) if s.cnt[f"qa{i}"]])
                blkf = fin.enter_context(nc.Block())
                s.flush(blkf)

        if upto.startswith('b0'):
            _fin(True)
            return nc

        PAIRN[0] = 4
        ffn_phase(x_in, x1b, True, '1')
        if upto == 'p1':
            _fin(True)
            return nc

        p2 = contextlib.ExitStack()
        with p2:
            s.dma_fence()
            sb2 = lambda name, shape, d=F32: p2.enter_context(nc.sbuf_tensor(name, shape, d))
            s_U = sb2("c_U", [128, 32 * 1024], BF16)
            s_Ur = sb2("c_Ur", [128, 32 * 1024], BF16)
            s_st = sb2("c_st", [128, 2 * 1024], BF16)
            s_ti = sb2("c_ti", [128, 4 * 1024], BF16)
            s_tp = sb2("c_tp", [128, 2 * 1024], BF16)
            s_cv = sb2("c_cv", [128, 2 * 1024], BF16)

            cv_tok = [None, None]
            ti_tok = [None] * 4
            tp_tok = [None, None]
            fence = []

            for g in range(2):
                tU = s.dma("sync", AP(s_U, 0, [[32 * 1024, 128], [1, 32 * 1024]]),
                           AP(ub, g * 128 * 32768, [[32768, 128], [1, 32 * 1024]]),
                           fence)
                jrev = []
                for t in range(32):
                    pj_, pji = pair(t)
                    a = s.do("tensor", lambda e, t=t, p=pj_: e.matmul(p[:, 0:512], p_j[:, :], s_U[:, t * 1024:t * 1024 + 512], start=True, stop=True), [tU, t_j] + ptok(pji) + fence, "tv")
                    a2 = s.do("tensor", lambda e, t=t, p=pj_: e.matmul(p[:, 512:1024], p_j[:, :], s_U[:, t * 1024 + 512:(t + 1) * 1024], start=True, stop=True), [], "tv")
                    b = s.do("vector", lambda e, t=t, p=pj_: e.tensor_copy(s_Ur[:, t * 1024:(t + 1) * 1024], p[:, :]), [a2] + fence, "vv")
                    psum_tok[pji] = b
                    jrev.append(b)
                jdone = jrev[-1]

                ibanks = [(pp[0], 0), (pp[0], 512), (pp[1], 0), (pp[1], 512),
                          (pp[2], 0), (pp[2], 512), (pp[3], 0), (pp[3], 512)]
                intra_done = None
                for hg in range(128):
                    buf = hg % 2
                    tT = s.dma("gpsimd", AP(s_tp, buf * 1024, [[2 * 1024, 128], [128, 8], [1, 128]]),
                               AP(kpad, hg * 8 * 256 + 1, [[1, 128], [256, 8], [1, 128]]),
                               [tp_tok[buf]] if tp_tok[buf] else [])
                    for hi in range(8):
                        h = hg * 8 + hi
                        slot = h % 8
                        pb, off = ibanks[slot]
                        w0 = [tT, jdone] + ([ps1_slots[slot]] if ps1_slots[slot] else [])
                        if hg == 0:
                            w0 += ptok(0) + ptok(1) + ptok(2) + ptok(3)
                        a = s.do("tensor", lambda e, h=h, buf=buf, hi=hi, pb=pb, off=off: e.matmul(
                            pb[:, off:off + 32],
                            s_tp[:, buf * 1024 + hi * 128:buf * 1024 + (hi + 1) * 128],
                            AP(s_Ur, h, [[32 * 1024, 128], [1024, 32]]), start=True, stop=True),
                            w0, "tv")
                        b = s.do("vector", lambda e, h=h, pb=pb, off=off: e.tensor_copy(
                            AP(s_Ur, h, [[32 * 1024, 128], [1024, 32]]), pb[:, off:off + 32]), [a], "vv")
                        ps1_slots[slot] = b
                        intra_done = b
                    tp_tok[buf] = intra_done
                psum_tok[0] = intra_done
                psum_tok[1] = intra_done
                psum_tok[2] = intra_done
                psum_tok[3] = intra_done

                c0_toks = []
                for bl in range(2):
                    tau = bl * NCH
                    gt = (g * 2 + bl) * NCH
                    c0_toks.append(s.dma("gpsimd", AP(cvb, (gt // 8) * 128 * 8192 + (gt % 8) * 1024, [[8192, 128], [1, 1024]]),
                                         s_Ur[:, tau * 1024:(tau + 1) * 1024], [intra_done]))

                sum_last = []
                for bl in range(2):
                    st_tok = None
                    for c in range(NCH - 1):
                        tau = bl * NCH + c
                        pS, pSi = pair(40 + bl)
                        w0 = ptok(pSi)
                        first = (c == 0)
                        if not first:
                            s.do("tensor", lambda e, p=pS, bl=bl: e.matmul(p[:, 0:512], p_ml[:, :], s_st[:, bl * 1024:bl * 1024 + 512], start=True, stop=False), [st_tok] + w0, "tv")
                            s.do("tensor", lambda e, p=pS, bl=bl: e.matmul(p[:, 512:1024], p_ml[:, :], s_st[:, bl * 1024 + 512:(bl + 1) * 1024], start=True, stop=False), [], "tv")
                        s.do("tensor", lambda e, p=pS, tau=tau, first=first: e.matmul(p[:, 0:512], p_wsum[:, :], s_U[:, tau * 1024:tau * 1024 + 512], start=first, stop=True), ([tU] + w0) if first else [tU], "tv")
                        mmt = s.do("tensor", lambda e, p=pS, tau=tau, first=first: e.matmul(p[:, 512:1024], p_wsum[:, :], s_U[:, tau * 1024 + 512:(tau + 1) * 1024], start=first, stop=True), [], "tv")
                        ev1 = s.do("vector", lambda e, p=pS, bl=bl: e.tensor_copy(s_st[:, bl * 1024:(bl + 1) * 1024], p[:, :]), [mmt], "vv")
                        tis = bl * 2 + (c % 2)
                        ev2 = s.do("vector", lambda e, p=pS, t=tis: e.tensor_mul(s_ti[:, t * 1024:(t + 1) * 1024], p_ct[:, :], p[:, :]), [mmt] + ([ti_tok[tis]] if ti_tok[tis] else []), "vv")
                        psum_tok[pSi] = ev2
                        st_tok = ev1
                        cc = c + 1
                        ctau = bl * NCH + cc
                        pC, pCi = pair(42 + (ctau % 2))
                        a = s.do("tensor", lambda e, p=pC, t=tis: e.matmul(p[:, 0:512], p_L[:, :], s_ti[:, t * 1024:t * 1024 + 512], start=True, stop=True), [ev2] + ptok(pCi), "tv")
                        a2 = s.do("tensor", lambda e, p=pC, t=tis: e.matmul(p[:, 512:1024], p_L[:, :], s_ti[:, t * 1024 + 512:(t + 1) * 1024], start=True, stop=True), [], "tv")
                        ti_tok[tis] = a2
                        cs = ctau % 2
                        b = s.do("vector", lambda e, p=pC, ctau=ctau, cs=cs: e.tensor_add(s_cv[:, cs * 1024:(cs + 1) * 1024], p[:, :], s_Ur[:, ctau * 1024:(ctau + 1) * 1024]), [a2, intra_done] + ([cv_tok[cs]] if cv_tok[cs] else []), "vv")
                        psum_tok[pCi] = b
                        gt = (g * 2 + bl) * NCH + cc
                        cv_tok[cs] = s.dma("gpsimd", AP(cvb, (gt // 8) * 128 * 8192 + (gt % 8) * 1024, [[8192, 128], [1, 1024]]),
                                           s_cv[:, cs * 1024:(cs + 1) * 1024], [b])
                    sum_last.append(ev2)

                fence = [cv_tok[0], cv_tok[1]] + c0_toks + sum_last

            blk2 = p2.enter_context(nc.Block())
            s.flush(blk2)

        if upto == 'p2':
            _fin(True)
            return nc

        p3 = contextlib.ExitStack()
        with p3:
            s.dma_fence()
            sb3 = lambda name, shape, d=F32: p3.enter_context(nc.sbuf_tensor(name, shape, d))
            a_x1 = sb3("a_x1", [128, MEGA * 8])
            a_cv = sb3("a_cv", [128, MEGA * 8], BF16)
            a_uT = sb3("a_uT", [128, MEGA * 8], BF16)
            a_cT = sb3("a_cT", [128, MEGA * 8], BF16)
            a_gg = sb3("a_gg", [128, MEGA * 8], BF16)
            a_gs = sb3("a_gs", [128, MEGA * 8], BF16)
            a_w = sb3("a_w", [128, 3 * 8 * 1024], BF16)
            a_bb = sb3("a_bb", [128, 3 * 1024])
            a_u = sb3("a_u", [128, 1024], BF16)
            a_pT = sb3("a_pT", [128, 2 * 1024], BF16)
            a_x2 = sb3("a_x2", [128, 2 * 1024])
            a_t = sb3("a_t", [128, 1024])
            a_br = sb3("a_br", [1, 1024])

            tw_g = s.dma("sync", AP(a_w, 0, [[3 * 8192, 128], [1024, 8], [1, 1024]]),
                         AP(gWT, 0, [[H, 128], [128 * H, 8], [1, H]]))
            tw_s = s.dma("sync", AP(a_w, 8192, [[3 * 8192, 128], [1024, 8], [1, 1024]]),
                         AP(ssWT, 0, [[H, 128], [128 * H, 8], [1, H]]))
            tw_f = s.dma("sync", AP(a_w, 2 * 8192, [[3 * 8192, 128], [1024, 8], [1, 1024]]),
                         AP(fWT, 0, [[H, 128], [128 * H, 8], [1, H]]))
            bias_tok = None
            prev_b = None
            for bi, bsrc in ((0, gb), (1, ssb), (2, fb)):
                tb = s.dma("sync", a_br[0:1, :], bsrc.ap(), [prev_b] if prev_b else [])
                for half in range(2):
                    pB, pBi = pair(bi * 2 + half)
                    a = s.do("tensor", lambda e, p=pB, h=half: e.matmul(p[:, h * 512:(h + 1) * 512], p_onr[:, :], a_br[0:1, h * 512:(h + 1) * 512], start=True, stop=True), [tb, t_onr] + ptok(pBi), "tv")
                    bias_tok = s.do("vector", lambda e, p=pB, bi=bi, h=half: e.tensor_copy(a_bb[:, bi * 1024 + h * 512:bi * 1024 + (h + 1) * 512], p[:, h * 512:(h + 1) * 512]), [a], "vv")
                    psum_tok[pBi] = bias_tok
                prev_b = a

            x1_free = None
            x2_tok = [None, None]
            pT_tok = [None, None]
            prev_uT = []
            prev_r2 = None
            for m in range(NMEGA):
                tx1 = s.dma("sync", AP(a_x1, 0, [[MEGA * 8, 128], [4096, 2], [1, 4096]]),
                            AP(x1b, 2 * m * 128 * 4096, [[4096, 128], [128 * 4096, 2], [1, 4096]]),
                            [x1_free] if x1_free else [])
                tcv = s.dma("sync", AP(a_cv, 0, [[MEGA * 8, 128], [1, 8192]]),
                            AP(cvb, m * 128 * 8192, [[8192, 128], [1, 8192]]),
                            [x1_free] if x1_free else [])
                uT_all = {}
                cT_all = {}
                for r in range(RPM):
                    ti = m * RPM + r
                    j1 = s.do("scalar", lambda e, r=r, ti=ti: e.activation(a_u[:, :], a_x1[:, r * 1024:(r + 1) * 1024], AF.Identity, bias=p_b2[:, ti:ti + 1], scale=p_r2[:, ti:ti + 1]), [tx1] + (uT_all[r - 1] if r else prev_uT), "sv")
                    uT_all[r] = [s.dma("scalar",
                        AP(a_uT, r * 1024, [[MEGA * 8, 128], [128, 8], [1, 128]]),
                        AP(a_u, 0, [[1024, 128], [128, 8], [1, 128]]), [j1], tr=True)]
                    cT_all[r] = [s.dma("scalar",
                        AP(a_cT, r * 1024, [[MEGA * 8, 128], [128, 8], [1, 128]]),
                        AP(a_cv, r * 1024, [[MEGA * 8, 128], [128, 8], [1, 128]]), [tcv], tr=True)]

                for r in range(RPM):
                    pG, pGi = pair(8 + 2 * r)
                    pS2, pSi2 = pair(9 + 2 * r)
                    mmg = mms_ = None
                    for j in range(8):
                        lu = a_uT[:, r * 1024 + j * 128:r * 1024 + (j + 1) * 128]
                        lc = a_cT[:, r * 1024 + j * 128:r * 1024 + (j + 1) * 128]
                        for half in range(2):
                            w0 = (uT_all[r] + cT_all[r] + [tw_g, tw_s] + ptok(pGi) + ptok(pSi2)) if (j == 0 and half == 0) else []
                            mmg = s.do("tensor", lambda e, l=lu, p=pG, j=j, h=half: e.matmul(p[:, h * 512:(h + 1) * 512], l, a_w[:, j * 1024 + h * 512:j * 1024 + (h + 1) * 512], start=(j == 0), stop=(j == 7)), w0, "tv")
                            mms_ = s.do("tensor", lambda e, l=lc, p=pS2, j=j, h=half: e.matmul(p[:, h * 512:(h + 1) * 512], l, a_w[:, 8192 + j * 1024 + h * 512:8192 + j * 1024 + (h + 1) * 512], start=(j == 0), stop=(j == 7)), [], "tv")
                    q1 = s.do("vector", lambda e, p=pG: e.tensor_add(a_t[:, :], p[:, :], a_bb[:, 0:1024]), [mms_, bias_tok, prev_r2], "vv")
                    q2 = s.do("scalar", lambda e, r=r: e.activation(a_gg[:, r * 1024:(r + 1) * 1024], a_t[:, :], AF.Gelu), [q1], "sv")
                    psum_tok[pGi] = q2
                    q3 = s.do("vector", lambda e, p=pS2: e.tensor_add(a_t[:, :], p[:, :], a_bb[:, 1024:2048]), [q2], "vv")
                    q4 = s.do("scalar", lambda e, r=r: e.activation(a_gs[:, r * 1024:(r + 1) * 1024], a_t[:, :], AF.Gelu), [q3], "sv")
                    psum_tok[pSi2] = q4
                    q5 = s.do("vector", lambda e, r=r: e.tensor_mul(a_gg[:, r * 1024:(r + 1) * 1024], a_gg[:, r * 1024:(r + 1) * 1024], a_gs[:, r * 1024:(r + 1) * 1024]), [q4], "vv")
                    pF, pFi = pair(30 + r)
                    mmf = None
                    qs_ = r % 2
                    bb = s.dma("scalar",
                               AP(a_pT, qs_ * 1024, [[2 * 1024, 128], [128, 8], [1, 128]]),
                               AP(a_gg, r * 1024, [[MEGA * 8, 128], [128, 8], [1, 128]]),
                               [q5, tw_f] + ([pT_tok[qs_]] if pT_tok[qs_] else []), tr=True)
                    for j in range(8):
                        for half in range(2):
                            w0 = ([bb] + ptok(pFi)) if (j == 0 and half == 0) else []
                            mmf = s.do("tensor", lambda e, q=qs_, p=pF, h=half, j=j: e.matmul(p[:, h * 512:(h + 1) * 512], a_pT[:, q * 1024 + j * 128:q * 1024 + (j + 1) * 128], a_w[:, 2 * 8192 + j * 1024 + h * 512:2 * 8192 + j * 1024 + (h + 1) * 512], start=(j == 0), stop=(j == 7)), w0, "tv")
                    pT_tok[qs_] = mmf
                    xr = r % 2
                    r1 = s.do("vector", lambda e, p=pF: e.tensor_add(a_t[:, :], p[:, :], a_bb[:, 2048:3072]), [mmf, q4], "vv")
                    r2_ = s.do("vector", lambda e, r=r, xr=xr: e.tensor_add(a_x2[:, xr * 1024:(xr + 1) * 1024], a_t[:, :], a_x1[:, r * 1024:(r + 1) * 1024]), [r1] + ([x2_tok[xr]] if x2_tok[xr] else []), "vv")
                    psum_tok[pFi] = r2_
                    prev_r2 = r2_
                    ti = m * RPM + r
                    x2_tok[xr] = s.dma("gpsimd", AP(x2b, (ti // 4) * 128 * 4096 + (ti % 4) * 1024, [[4096, 128], [1, 1024]]),
                                       a_x2[:, xr * 1024:(xr + 1) * 1024], [r2_])
                x1_free = x2_tok[1] or x2_tok[0]
                prev_uT = uT_all[RPM - 1]

            blk3 = p3.enter_context(nc.Block())
            s.flush(blk3)

        if upto == 'p3a':
            _fin(True)
            return nc

        ffn_phase(x2b, y_out, False, '2')
        _fin()

    return nc


_NC_CACHE = {}


def tile_x(x_core):
    return np.ascontiguousarray(
        x_core.reshape(16, 4, 128, 1024).transpose(0, 2, 1, 3).reshape(TC, H))


def untile_y(y_core):
    return np.ascontiguousarray(
        y_core.reshape(16, 128, 4, 1024).transpose(0, 2, 1, 3).reshape(TC, H))


def kernel(**inputs):
    x = np.ascontiguousarray(np.asarray(inputs["L0_attn_x"], dtype=np.float32))
    assert x.shape == (32, L, H)
    xf = x.reshape(32, L * H)

    if "nc" not in _NC_CACHE:
        _NC_CACHE["nc"] = build_program()
    nc = _NC_CACHE["nc"]

    iotas = np.stack([np.arange(128, dtype=np.float32),
                      np.arange(1, 129, dtype=np.float32),
                      np.arange(127, -1, -1, dtype=np.float32)], axis=1)
    shared = {
        "in_proj": np.ascontiguousarray(inputs["L0_attn_in_proj"], np.float32),
        "out_proj": np.ascontiguousarray(inputs["L0_attn_out_proj"], np.float32),
        "frequencies": np.ascontiguousarray(inputs["L0_attn_frequencies"], np.float32).reshape(1, NST),
        "decays": np.ascontiguousarray(inputs["L0_attn_decays"], np.float32).reshape(1, NST),
        "ss_W": np.ascontiguousarray(inputs["L0_attn_ss_W"], np.float32),
        "gate_W": np.ascontiguousarray(inputs["L0_attn_gate_W"], np.float32),
        "fin_W": np.ascontiguousarray(inputs["L0_attn_fin_W"], np.float32),
        "ss_b": np.ascontiguousarray(inputs["L0_attn_ss_b"], np.float32).reshape(1, H),
        "gate_b": np.ascontiguousarray(inputs["L0_attn_gate_b"], np.float32).reshape(1, H),
        "fin_b": np.ascontiguousarray(inputs["L0_attn_fin_b"], np.float32).reshape(1, H),
        "w_gate": np.ascontiguousarray(inputs["L0_ffn_w_gate"], np.float32),
        "w_up": np.ascontiguousarray(inputs["L0_ffn_w_up"], np.float32),
        "w_down": np.ascontiguousarray(inputs["L0_ffn_w_down"], np.float32),
        "c_ident": np.eye(128, dtype=BFNP),
        "c_jrev": np.eye(128, dtype=BFNP)[::-1].copy(),
        "c_ones_r": np.ones((1, 128), dtype=np.float32),
        "c_ones_c": np.ones((128, 1), dtype=np.float32),
        "c_iotas": np.ascontiguousarray(iotas),
    }
    in_maps = []
    for c in range(NCORES):
        im = dict(shared)
        im["x"] = tile_x(xf[c * B_CORE:(c + 1) * B_CORE].reshape(TC, H))
        in_maps.append(im)

    res = run_bass_kernel_spmd(nc, in_maps, core_ids=list(range(NCORES)))
    out = np.empty((32, L, H), dtype=np.float32)
    for c in range(NCORES):
        out[c * B_CORE:(c + 1) * B_CORE] = untile_y(
            np.asarray(res.results[c]["y"]).reshape(TC, H)).reshape(B_CORE, L, H)
    return out
